# version 07
# baseline (speedup 1.0000x reference)
import numpy as np
import ml_dtypes
import concourse.bass as bass
import concourse.mybir as mybir
from concourse.bass_utils import run_bass_kernel_spmd
from concourse.bass import IndirectOffsetOnAxis
from contextlib import ExitStack

F32 = mybir.dt.float32
BF16 = mybir.dt.bfloat16
I32 = mybir.dt.int32
AF = mybir.ActivationFunctionType
ALU = mybir.AluOpType
NPBF = ml_dtypes.bfloat16
ds = bass.ds

C = 4096
T = 4096
BATCH = 2
TCORE = 1024
H = 32
NKC = C // 128
NIC = 11008 // 128
NT = TCORE // 128
SCALE = 128.0 ** -0.5
EPS = 1e-5
GT = NKC * NIC
TPC = GT // 8
PREPG = 4
NGRP = 3 * TPC // PREPG
ENGINES = ("tensor", "vector", "scalar", "sync", "gpsimd")



def _pairperm():
    src = np.arange(C).reshape(H, 64, 2).transpose(0, 2, 1)
    return src.reshape(-1)


def _tile_attn_w(w, perm=None):
    if perm is not None:
        w = w[perm]
    wt = np.ascontiguousarray(w.T)
    t = wt.reshape(NKC, 128, NKC, 128).transpose(2, 1, 0, 3)
    return np.ascontiguousarray(t.reshape(NKC, 128, C))


def _tile_q_gateup(q):
    qt = np.ascontiguousarray(q.T)
    t = qt.reshape(NIC, 128, NKC, 128).transpose(0, 2, 1, 3)
    return np.ascontiguousarray(t.reshape(GT, 128, 128))


def _tile_q_down(q):
    qt = np.ascontiguousarray(q.T)
    t = qt.reshape(NKC, 128, NIC, 128).transpose(0, 2, 1, 3)
    return np.ascontiguousarray(t.reshape(GT, 128, 128))


def _sz_gateup(s):
    st = s.T.reshape(NIC, 128, NKC).transpose(0, 2, 1)
    return np.ascontiguousarray(st.reshape(GT, 128))


def _sz_down(s):
    st = s.T.reshape(NKC, 128, NIC).transpose(0, 2, 1)
    return np.ascontiguousarray(st.reshape(GT, 128))


def _host_prep(inp):
    x = np.asarray(inp["L0_attn_x"], dtype=np.float32)
    perm = _pairperm()
    wq_t = _tile_attn_w(np.asarray(inp["L0_attn_wq"], np.float32), perm)
    wk_t = _tile_attn_w(np.asarray(inp["L0_attn_wk"], np.float32), perm)
    wv_t = _tile_attn_w(np.asarray(inp["L0_attn_wv"], np.float32))
    wo_t = _tile_attn_w(np.asarray(inp["L0_attn_wo"], np.float32))

    g_tiles = _tile_q_gateup(np.asarray(inp["L0_ffn_gate_q"], np.int32))
    u_tiles = _tile_q_gateup(np.asarray(inp["L0_ffn_up_q"], np.int32))
    d_tiles = _tile_q_down(np.asarray(inp["L0_ffn_down_q"], np.int32))
    g_s = _sz_gateup(np.asarray(inp["L0_ffn_gate_s"], np.float32))
    g_z = _sz_gateup(np.asarray(inp["L0_ffn_gate_z"], np.float32))
    u_s = _sz_gateup(np.asarray(inp["L0_ffn_up_s"], np.float32))
    u_z = _sz_gateup(np.asarray(inp["L0_ffn_up_z"], np.float32))
    d_s = _sz_down(np.asarray(inp["L0_ffn_down_s"], np.float32))
    d_z = _sz_down(np.asarray(inp["L0_ffn_down_z"], np.float32))

    masks = np.zeros((4, 128, 512), dtype=NPBF)
    uu = np.arange(128)[:, None]
    tt = np.arange(512)[None, :]
    for w in range(4):
        masks[w] = (uu + 128 * w <= tt).astype(NPBF)

    in_maps = []
    for c in range(8):
        b, j = c // 4, c % 4
        blocks = (j, j + 4)
        xs = np.concatenate([x[b, 512 * m:512 * (m + 1)] for m in blocks], axis=0)
        gidx = np.concatenate([np.arange(512 * m, 512 * (m + 1)) for m in blocks])
        cosr = np.cos(gidx.astype(np.float64)).astype(NPBF)
        sinr = np.sin(gidx.astype(np.float64)).astype(NPBF)
        cosb = np.ascontiguousarray(np.broadcast_to(cosr, (128, TCORE)))
        sinb = np.concatenate([np.broadcast_to(-sinr, (64, TCORE)),
                               np.broadcast_to(sinr, (64, TCORE))], axis=0)
        sinb = np.ascontiguousarray(sinb.astype(NPBF))
        sl = slice(TPC * c, TPC * (c + 1))
        qtiles = np.concatenate([g_tiles[sl], u_tiles[sl], d_tiles[sl]], axis=0)
        qtiles = np.ascontiguousarray(
            qtiles.reshape(3 * TPC // PREPG, PREPG, 128, 128)
            .transpose(0, 2, 1, 3).reshape(3 * TPC // PREPG, 128, PREPG * 128))
        dq_s = np.concatenate([g_s[sl], u_s[sl], d_s[sl]], axis=0).T
        dq_z = np.concatenate([g_z[sl], u_z[sl], d_z[sl]], axis=0).T
        asm_idx = np.zeros((128, 96), np.int32)
        for su in range(8):
            for r in range(4):
                base = 1024 * j + 128 * su + np.arange(128)
                asm_idx[:, su * 12 + r] = 4096 * (4 * b + r) + base
                asm_idx[:, su * 12 + 4 + r] = 8192 * (4 * b + r) + base
                asm_idx[:, su * 12 + 8 + r] = 8192 * (4 * b + r) + 4096 + base
        in_maps.append(dict(
            x_slice=np.ascontiguousarray(xs),
            asm_idx=asm_idx,
            wq_t=wq_t, wk_t=wk_t, wv_t=wv_t, wo_t=wo_t,
            gq=np.ascontiguousarray(qtiles),
            deq_s=np.ascontiguousarray(dq_s), deq_z=np.ascontiguousarray(dq_z),
            rope_cos=cosb, rope_sin=sinb, mask01=masks,
            ones_col=np.ones((128, 1), dtype=NPBF),
            ones_row=np.ones((1, 128), dtype=NPBF),
            eye=np.eye(128, dtype=NPBF),
            jmat=np.ascontiguousarray(np.roll(np.eye(128, dtype=NPBF), 64, axis=1)),
        ))
    return in_maps



class Sched:
    def __init__(self):
        self.steps = {e: [] for e in ENGINES}
        self.plan = {}
        self.isdma = {}
        self.owner = {}

    def sem(self, name, dma=False):
        assert name not in self.plan
        self.plan[name] = 0
        self.isdma[name] = dma
        self.owner[name] = None
        return name

    def cur(self, name):
        return (name, self.plan[name])

    def emit(self, eng, fn, waits=(), inc=None):
        ev = None
        if inc is not None:
            amt = 16 if self.isdma[inc] else 1
            if self.owner[inc] is None:
                self.owner[inc] = eng
            assert self.owner[inc] == eng, f"{inc}: {self.owner[inc]} vs {eng}"
            self.plan[inc] += amt
            ev = (inc, self.plan[inc], amt)
        ww = []
        for w in waits:
            if w is None:
                continue
            sn, v = w[0], w[1]
            if self.isdma[sn]:
                assert v == self.plan[sn], f"partial dma wait {sn}: {v} != {self.plan[sn]}"
            ww.append((sn, v))
        self.steps[eng].append((fn, tuple(ww), ev))
        return (ev[0], ev[1]) if ev else None

    def run(self, eng, E, SM, RG):
        lastw = {}
        for idx, (fn, waits, ev) in enumerate(self.steps[eng]):
            for (sn, v) in waits:
                if lastw.get(sn, -1) < v:
                    E.wait_ge(SM[sn], v)
                    lastw[sn] = v
            try:
                inst = fn(E, SM, RG)
            except Exception:
                import sys
                print(f"Sched.run failed at {eng} step {idx}/{len(self.steps[eng])}",
                      file=sys.stderr)
                raise
            if ev is not None:
                sn, _v, amt = ev
                if amt == 16:
                    inst.then_inc(SM[sn], 16)
                else:
                    inst.then_inc(SM[sn])



def _build():
    nc = bass.Bass("TRN2", target_bir_lowering=False)
    x_in = nc.dram_tensor("x_slice", [TCORE, C], F32, kind="ExternalInput").ap()
    w_t = {n: nc.dram_tensor(n, [NKC, 128, C], F32, kind="ExternalInput").ap()
           for n in ("wq_t", "wk_t", "wv_t", "wo_t")}
    gq = nc.dram_tensor("gq", [NGRP, 128, PREPG * 128], I32, kind="ExternalInput").ap()
    asm_idx_d = nc.dram_tensor("asm_idx", [128, 96], I32, kind="ExternalInput").ap()
    deq_s_d = nc.dram_tensor("deq_s", [128, 3 * TPC], F32, kind="ExternalInput").ap()
    deq_z_d = nc.dram_tensor("deq_z", [128, 3 * TPC], F32, kind="ExternalInput").ap()
    cos_d = nc.dram_tensor("rope_cos", [128, TCORE], BF16, kind="ExternalInput").ap()
    sin_d = nc.dram_tensor("rope_sin", [128, TCORE], BF16, kind="ExternalInput").ap()
    mask_d = nc.dram_tensor("mask01", [4, 128, 512], BF16, kind="ExternalInput").ap()
    oc_d = nc.dram_tensor("ones_col", [128, 1], BF16, kind="ExternalInput").ap()
    or_d = nc.dram_tensor("ones_row", [1, 128], BF16, kind="ExternalInput").ap()
    eye_d = nc.dram_tensor("eye", [128, 128], BF16, kind="ExternalInput").ap()
    jmat_d = nc.dram_tensor("jmat", [128, 128], BF16, kind="ExternalInput").ap()
    out_d = nc.dram_tensor("out", [TCORE, C], F32, kind="ExternalOutput").ap()

    xbf_d = nc.dram_tensor("xbf", [TCORE, C], BF16).ap()
    x0fm_d = nc.dram_tensor("x0fm", [C, TCORE], BF16).ap()
    x1_d = nc.dram_tensor("x1d", [C, TCORE], BF16).ap()
    x2_d = nc.dram_tensor("x2d", [C, TCORE], BF16).ap()
    kvq_con = nc.dram_tensor("kvq_con", [3 * C, TCORE], BF16).ap()
    kv_gat = nc.dram_tensor("kv_gat", [8 * 2 * C, TCORE], BF16, addr_space="Shared").ap()
    q_gat = nc.dram_tensor("q_gat", [8 * C, TCORE], BF16, addr_space="Shared").ap()
    wconA = nc.dram_tensor("wconA", [128, 2 * TPC * 128], BF16).ap()
    wgatA = nc.dram_tensor("wgatA", [8 * 128, 2 * TPC * 128], BF16, addr_space="Shared").ap()
    wconB = nc.dram_tensor("wconB", [128, TPC * 128], BF16).ap()
    wgatB = nc.dram_tensor("wgatB", [8 * 128, TPC * 128], BF16, addr_space="Shared").ap()
    ao_con = nc.dram_tensor("ao_con", [8 * 128, T], BF16).ap()
    ao_gat = nc.dram_tensor("ao_gat", [8 * 8 * 128, T], BF16, addr_space="Shared").ap()
    h_d = nc.dram_tensor("h_d", [NIC, 128, 1024], BF16).ap()
    fo_d = nc.dram_tensor("fo_d", [2, NKC, 128, 512], BF16).ap()

    S = Sched()
    sm = lambda n: S.sem(n, dma=False)
    dm = lambda n: S.sem(n, dma=True)

    st = ExitStack()
    with st:
        E_ = st.enter_context
        A = E_(nc.sbuf_tensor("A", [128, 32768], BF16))
        B = E_(nc.sbuf_tensor("Bb", [128, 32768], BF16))
        WS = E_(nc.sbuf_tensor("WS", [128, 11264], BF16))
        PQ = E_(nc.sbuf_tensor("PQ", [128, 1024], I32))
        PD = E_(nc.sbuf_tensor("PD", [128, 1024], BF16))
        PN = E_(nc.sbuf_tensor("PN", [128, 1024], BF16))
        s_ds = E_(nc.sbuf_tensor("s_ds", [128, 3 * TPC], F32))
        s_dz = E_(nc.sbuf_tensor("s_dz", [128, 3 * TPC], F32))
        s_cos = E_(nc.sbuf_tensor("s_cos", [128, TCORE], BF16))
        s_sin = E_(nc.sbuf_tensor("s_sin", [128, TCORE], BF16))
        s_oc = E_(nc.sbuf_tensor("s_oc", [128, 1], BF16))
        s_or = E_(nc.sbuf_tensor("s_or", [1, 128], BF16))
        s_eye = E_(nc.sbuf_tensor("s_eye", [128, 128], BF16))
        s_j = E_(nc.sbuf_tensor("s_j", [128, 128], BF16))
        s_mb = E_(nc.sbuf_tensor("s_mb", [128, 1024], BF16))
        s_rb = E_(nc.sbuf_tensor("s_rb", [128, 1024], BF16))
        s_rows = E_(nc.sbuf_tensor("s_rows", [1, 2048], F32))
        s_msq = E_(nc.sbuf_tensor("s_msq", [1, 1024], F32))
        s_rbf = E_(nc.sbuf_tensor("s_rbf", [1, 1024], BF16))
        s_bnd = E_(nc.sbuf_tensor("s_bnd", [1, 64], F32))
        s_eps = E_(nc.sbuf_tensor("s_eps", [1, 2], F32))
        s_bia = E_(nc.sbuf_tensor("s_bia", [128, 1], F32))
        s_aidx = E_(nc.sbuf_tensor("s_aidx", [128, 96], I32))
        s_qe = E_(nc.sbuf_tensor("s_qe", [128, 2048], BF16))
        s_qr = E_(nc.sbuf_tensor("s_qr", [128, 2048], BF16))
        s_sq = E_(nc.sbuf_tensor("s_sq", [128, 1024], BF16))
        s_tr1 = s_sq
        s_sil = E_(nc.sbuf_tensor("s_sil", [128, 1024], BF16))
        s_hst = E_(nc.sbuf_tensor("s_hst", [128, 1024], BF16))
        s_fst = E_(nc.sbuf_tensor("s_fst", [128, 1024], BF16))
        s_recb = E_(nc.sbuf_tensor("s_recb", [128, 512], BF16))

        s_pb = B[:, 0:4096]
        s_ao = B[:, 4096:5120]
        s_msk = B[:, 5120:7168]
        s_x0s = s_hst
        s_fx3 = (s_qe, s_qr)

        P_BIG = E_(nc.psum_tensor("P_BIG", [128, 2048], F32))
        P_MED = E_(nc.psum_tensor("P_MED", [128, 1024], F32))
        P_BF = E_(nc.psum_tensor("P_BF", [128, 1024], BF16))
        P_AUX = E_(nc.psum_tensor("P_AUX", [128, 512], F32))

        ld_const = dm("ld_const")
        zp_done = sm("zp_done")
        ld_prep = [dm(f"ld_prep{i}") for i in range(2)]
        prep_deq = sm("prep_deq")
        prep_tp = sm("prep_tp")
        prep_ev = sm("prep_ev")
        prep_st = dm("prep_st")
        cc_done = sm("cc_done")
        ld_x = [dm(f"ld_x{i}") for i in range(2)]
        st_xbf = dm("st_xbf")
        ld_xfm = dm("ld_xfm")
        sq_done = sm("sq_done")
        stat_mm = sm("stat_mm")
        ln_fin = sm("ln_fin")
        bc_ev = sm("bc_ev")
        norm_done = sm("norm_done")
        st_x0 = dm("st_x0")
        ld_w = [dm(f"ld_w{i}") for i in range(2)]
        qkv_mm = sm("qkv_mm")
        qkv_ev = sm("qkv_ev")
        rope_done = sm("rope_done")
        st_con = [dm("st_con0"), dm("st_con1")]
        ld_slot = [dm(f"ld_slot{i}") for i in range(2)]
        vtp = sm("vtp")
        vtp_ev = sm("vtp_ev")
        bnd_sq = sm("bnd_sq")
        bnd_mm = sm("bnd_mm")
        bnd_ev = sm("bnd_ev")
        bnd_red = sm("bnd_red")
        att_s = sm("att_s")
        att_e = sm("att_e")
        att_m = sm("att_m")
        att_pv = sm("att_pv")
        att_ev = sm("att_ev")
        rs_mm = sm("rs_mm")
        rs_cp = sm("rs_cp")
        rs_ve = sm("rs_ve")
        st_ao = dm("st_ao")
        ld_attn = dm("ld_attn")
        o_mm = sm("o_mm")
        ld_x0s = [dm("ld_x0s0")]
        res1 = sm("res1")
        st_x1 = dm("st_x1")
        ld_fg = [dm(f"ld_fg{i}") for i in range(2)]
        f1_mm = sm("f1_mm")
        f1_sil = sm("f1_sil")
        f1_mul = sm("f1_mul")
        st_h = dm("st_h")
        ld_h = dm("ld_h")
        ld_wd = [dm(f"ld_wd{i}") for i in range(2)]
        f2_mm = sm("f2_mm")
        f2_ev = sm("f2_ev")
        st_fo = dm("st_fo")
        ld_fx = [dm(f"ld_fx{i}") for i in range(2)]
        f3_v = sm("f3_v")
        st_x2 = dm("st_x2")
        ld_out = [dm(f"ld_out{i}") for i in range(2)]
        st_out = dm("st_out")

        for (dst, src) in ((s_ds, deq_s_d), (s_dz, deq_z_d), (s_cos, cos_d),
                           (s_sin, sin_d), (s_oc, oc_d), (s_or, or_d),
                           (s_eye, eye_d), (s_j, jmat_d)):
            S.emit("gpsimd", lambda E, SM, RG, d=dst, s=src: E.dma_start(out=d[:], in_=s),
                   inc=ld_const)
        S.emit("gpsimd", lambda E, SM, RG: E.dma_start(
            out=s_aidx[:], in_=asm_idx_d), inc=ld_const)
        ev_const = S.cur(ld_const)
        ev_zp = S.emit("vector", lambda E, SM, RG: E.scalar_tensor_tensor(
            s_dz[:], s_ds[:], -8.0, s_dz[:], ALU.mult, ALU.add),
            waits=[ev_const], inc=zp_done)
        eps_done = sm("eps_done")
        S.emit("vector", lambda E, SM, RG: E.memset(s_eps[0:1, 0:1], EPS), inc=eps_done)
        ev_eps = S.emit("vector", lambda E, SM, RG: E.memset(s_eps[0:1, 1:2], 1e-12),
                        inc=eps_done)

        ev_xst = None
        for tc in range(NT):
            sp = tc % 2
            w = [S.cur(st_xbf)] if tc >= 2 else []
            ev_l = S.emit("gpsimd", lambda E, SM, RG, tc=tc, sp=sp: E.dma_start(
                out=A[:, 4096 * sp:4096 * (sp + 1)], in_=x_in[128 * tc:128 * (tc + 1), :]),
                waits=w, inc=ld_x[sp])
            ev_xst = S.emit("gpsimd", lambda E, SM, RG, tc=tc, sp=sp: E.dma_start(
                out=xbf_d[128 * tc:128 * (tc + 1), :], in_=A[:, 4096 * sp:4096 * (sp + 1)]),
                waits=[ev_l], inc=st_xbf)
        for k in range(NKC):
            S.emit("scalar", lambda E, SM, RG, k=k: E.dma_start_transpose(
                B[:, 1024 * k:1024 * (k + 1)], xbf_d[:, 128 * k:128 * (k + 1)]),
                waits=[ev_xst], inc=ld_xfm)
        ev_xfm_all = S.cur(ld_xfm)

        def LN_emit(src, wait_chunk=None, first_waits=()):
            prev_s2 = None
            for k in range(NKC):
                wsq = list(first_waits) if k == 0 else []
                if wait_chunk is not None:
                    wsq.append(wait_chunk(k))
                if prev_s2 is not None:
                    wsq.append(prev_s2)
                ev_sq = S.emit("scalar", lambda E, SM, RG, k=k: E.activation(
                    s_sq[:, 0:1024], src(k), AF.Square), waits=wsq, inc=sq_done)
                for hf in range(2):
                    S.emit("tensor", lambda E, SM, RG, k=k, hf=hf: E.matmul(
                        P_BIG[0:1, 512 * hf:512 * (hf + 1)], s_oc[:],
                        src(k)[:, 512 * hf:512 * (hf + 1)],
                        start=(k == 0), stop=(k == NKC - 1)),
                        waits=(wsq + [ev_const]) if hf == 0 else [], inc=stat_mm)
                for hf in range(2):
                    prev_s2 = S.emit("tensor", lambda E, SM, RG, k=k, hf=hf: E.matmul(
                        P_BIG[0:1, 1024 + 512 * hf:1024 + 512 * (hf + 1)], s_oc[:],
                        s_sq[:, 512 * hf:512 * (hf + 1)],
                        start=(k == 0), stop=(k == NKC - 1)),
                        waits=[ev_sq] if hf == 0 else [], inc=stat_mm)
            return LN_finalize(prev_s2)

        def LN_finalize(ev_mm):
            ev_f1 = S.emit("scalar", lambda E, SM, RG: E.activation(
                s_rows[:, 0:1024], P_BIG[0:1, 0:1024], AF.Copy, scale=1.0 / C),
                waits=[ev_mm], inc=bc_ev)
            ev_f2 = S.emit("scalar", lambda E, SM, RG: E.activation(
                s_rows[:, 1024:2048], P_BIG[0:1, 1024:2048], AF.Copy, scale=1.0 / C),
                inc=bc_ev)
            ev_a = S.emit("vector", lambda E, SM, RG: E.tensor_mul(
                s_msq[:], s_rows[0:1, 0:1024], s_rows[0:1, 0:1024]),
                waits=[ev_f1], inc=ln_fin)
            ev_b = S.emit("vector", lambda E, SM, RG: E.tensor_sub(
                s_msq[:], s_rows[0:1, 1024:2048], s_msq[:]),
                waits=[ev_f2, ev_a], inc=ln_fin)
            ev_c = S.emit("scalar", lambda E, SM, RG: E.activation(
                s_msq[:], s_msq[:], AF.Ln, bias=s_eps[0:1, 0:1]),
                waits=[ev_b, ev_eps], inc=bc_ev)
            ev_d = S.emit("scalar", lambda E, SM, RG: E.activation(
                s_msq[:], s_msq[:], AF.Exp, scale=-0.5), inc=bc_ev)
            ev_e = S.emit("vector", lambda E, SM, RG: E.tensor_mul(
                s_rows[0:1, 0:1024], s_rows[0:1, 0:1024], s_msq[:]),
                waits=[ev_d], inc=ln_fin)
            ev_g = S.emit("vector", lambda E, SM, RG: E.tensor_copy(
                s_rbf[0:1, 0:1024], s_msq[0:1, 0:1024]), waits=[ev_d], inc=ln_fin)
            ev_prev = None
            for hf in range(2):
                wv_ = [ev_g]
                if ev_prev is not None:
                    wv_.append(ev_prev)
                ev_bm = S.emit("tensor", lambda E, SM, RG, hf=hf: E.matmul(
                    P_AUX[:], s_or[:], s_rbf[0:1, 512 * hf:512 * (hf + 1)],
                    start=True, stop=True), waits=wv_, inc=stat_mm)
                ev_prev = S.emit("scalar", lambda E, SM, RG, hf=hf: E.activation(
                    s_rb[:, 512 * hf:512 * (hf + 1)], P_AUX[:], AF.Copy),
                    waits=[ev_bm], inc=bc_ev)
            ev_h = S.emit("vector", lambda E, SM, RG: E.tensor_copy(
                s_rbf[0:1, 0:1024], s_rows[0:1, 0:1024]),
                waits=[ev_e, ev_prev], inc=ln_fin)
            for hf in range(2):
                ev_bm = S.emit("tensor", lambda E, SM, RG, hf=hf: E.matmul(
                    P_AUX[:], s_or[:], s_rbf[0:1, 512 * hf:512 * (hf + 1)],
                    start=True, stop=True), waits=[ev_h, ev_prev], inc=stat_mm)
                ev_prev = S.emit("scalar", lambda E, SM, RG, hf=hf: E.activation(
                    s_mb[:, 512 * hf:512 * (hf + 1)], P_AUX[:], AF.Copy),
                    waits=[ev_bm], inc=bc_ev)
            return ev_prev

        def LN_norm(src_ap, dst_ap, ev_bc, extra=()):
            S.emit("vector", lambda E, SM, RG: E.tensor_mul(
                dst_ap, src_ap, s_rb[:, 0:1024]),
                waits=[ev_bc] + list(extra), inc=norm_done)
            return S.emit("vector", lambda E, SM, RG: E.tensor_sub(
                dst_ap, dst_ap, s_mb[:, 0:1024]), inc=norm_done)

        ev_bc1 = LN_emit(lambda k: B[:, 1024 * k:1024 * (k + 1)],
                         first_waits=[ev_xfm_all])
        for k in range(NKC):
            ev_ln1 = LN_norm(B[:, 1024 * k:1024 * (k + 1)],
                             A[:, 1024 * k:1024 * (k + 1)], ev_bc1)
        ev_x0sp = S.emit("scalar", lambda E, SM, RG: E.dma_start(
            out=x0fm_d[:].rearrange("(k p) t -> p k t", p=128),
            in_=B[:].rearrange("p (k t) -> p k t", k=NKC)),
            waits=[ev_ln1], inc=st_x0)

        def w_slot(sp):
            return WS[:, 4096 * sp: 4096 * (sp + 1)]

        wstate = {"count": 0, "slot_user": [None, None]}

        def load_wblock(name, n):
            sp = wstate["count"] % 2
            w = [wstate["slot_user"][sp]] if wstate["slot_user"][sp] else []
            ev = S.emit("gpsimd", lambda E, SM, RG, name=name, n=n, sp=sp: E.dma_start(
                out=w_slot(sp), in_=w_t[name][n]),
                waits=w, inc=ld_w[sp])
            wstate["count"] += 1
            return ev, sp

        pstate = {"slot_user": [None, None]}

        def proj_psum_waits(qp):
            return [pstate["slot_user"][qp]] if pstate["slot_user"][qp] else []

        gidx = [0]
        qkvs = {"qe_user": [None, None], "qr_user": [None, None],
                "jpsum_user": [None, None], "pending": None}

        def qkv_one(name, n, crow, rope):
            ev_w, sp = load_wblock(name, n)
            qp = gidx[0] % 2
            gidx[0] += 1
            mmw = [ev_w, ev_ln1] + proj_psum_waits(qp)
            for k in range(NKC):
                if k == 8 and qkvs["pending"] is not None:
                    qkvs["pending"]()
                    qkvs["pending"] = None
                for hf in range(2):
                    ev_mm = S.emit("tensor", lambda E, SM, RG, k=k, hf=hf, sp=sp, qp=qp: E.matmul(
                        P_BIG[:, 1024 * qp + 512 * hf:1024 * qp + 512 * (hf + 1)],
                        w_slot(sp)[:, 128 * k:128 * (k + 1)],
                        A[:, 1024 * k + 512 * hf:1024 * k + 512 * (hf + 1)],
                        start=(k == 0), stop=(k == NKC - 1)),
                        waits=mmw if (k == 0 and hf == 0) else [], inc=qkv_mm)
            wstate["slot_user"][sp] = ev_mm
            evw = [ev_mm, S.cur(st_con[qp])]
            if qkvs["qe_user"][qp] is not None:
                evw.append(qkvs["qe_user"][qp])
            ev_ev = S.emit("scalar", lambda E, SM, RG, qp=qp: E.activation(
                s_qe[:, 1024 * qp:1024 * (qp + 1)], P_BIG[:, 1024 * qp:1024 * (qp + 1)],
                AF.Copy), waits=evw, inc=qkv_ev)
            pstate["slot_user"][qp] = ev_ev
            if rope:
                def emit_jmm(qp=qp, ev_ev=ev_ev):
                    wj = [ev_ev, ev_const]
                    ju = qkvs["jpsum_user"][0]
                    if ju is not None:
                        wj.append(ju)
                    for hf in range(2):
                        ev_jmm = S.emit("tensor", lambda E, SM, RG, qp=qp, hf=hf: E.matmul(
                            P_MED[:, 512 * hf:512 * (hf + 1)], s_j[:],
                            s_qe[:, 1024 * qp + 512 * hf:1024 * qp + 512 * (hf + 1)],
                            start=True, stop=True),
                            waits=wj if hf == 0 else [], inc=qkv_mm)
                    wr = [ev_ev, S.cur(st_con[qp])]
                    S.emit("vector", lambda E, SM, RG, qp=qp: E.tensor_mul(
                        s_tr1[:, :], s_qe[:, 1024 * qp:1024 * (qp + 1)], s_cos[:]),
                        waits=wr, inc=rope_done)
                    ev_j2 = S.emit("vector", lambda E, SM, RG, qp=qp: E.tensor_mul(
                        s_qr[:, 1024 * qp:1024 * (qp + 1)], P_MED[:, 0:1024], s_sin[:]),
                        waits=[ev_jmm], inc=rope_done)
                    qkvs["jpsum_user"][0] = ev_j2
                    ev_src = S.emit("vector", lambda E, SM, RG, qp=qp: E.tensor_add(
                        s_qr[:, 1024 * qp:1024 * (qp + 1)],
                        s_qr[:, 1024 * qp:1024 * (qp + 1)], s_tr1[:, :]),
                        inc=rope_done)
                    qkvs["qe_user"][qp] = ev_src
                    ev_st = S.emit("sync", lambda E, SM, RG, crow=crow, n=n, qp=qp: E.dma_start(
                        out=kvq_con[crow + 128 * n: crow + 128 * (n + 1), :],
                        in_=s_qr[:, 1024 * qp:1024 * (qp + 1)]),
                        waits=[ev_src], inc=st_con[qp])
                qkvs["pending"] = emit_jmm
            else:
                ev_st = S.emit("sync", lambda E, SM, RG, crow=crow, n=n, qp=qp: E.dma_start(
                    out=kvq_con[crow + 128 * n: crow + 128 * (n + 1), :],
                    in_=s_qe[:, 1024 * qp:1024 * (qp + 1)]),
                    waits=[ev_ev], inc=st_con[qp])
                qkvs["qe_user"][qp] = None

        for n in range(NKC):
            qkv_one("wk_t", n, C, True)
        for n in range(NKC):
            qkv_one("wv_t", n, 2 * C, False)
        if qkvs["pending"] is not None:
            qkvs["pending"]()
            qkvs["pending"] = None
        ev_agKV = S.emit("gpsimd", lambda E, SM, RG: E.collective_compute(
            "AllGather", ALU.bypass, replica_groups=[[0, 1, 2, 3, 4, 5, 6, 7]],
            ins=[kvq_con[C:3 * C, :].opt()], outs=[kv_gat.opt()]),
            waits=[S.cur(st_con[0]), S.cur(st_con[1])], inc=cc_done)
        for n in range(NKC):
            qkv_one("wq_t", n, 0, True)
        if qkvs["pending"] is not None:
            qkvs["pending"]()
            qkvs["pending"] = None
        ev_agQ = S.emit("gpsimd", lambda E, SM, RG: E.collective_compute(
            "AllGather", ALU.bypass, replica_groups=[[0, 1, 2, 3, 4, 5, 6, 7]],
            ins=[kvq_con[0:C, :].opt()], outs=[q_gat.opt()]),
            waits=[S.cur(st_con[0]), S.cur(st_con[1])], inc=cc_done)
        ev_agA = ev_agQ
        pq_user = [None, None]
        pd_user = [None, None]
        pn_user = [None, None]
        for grp in range(NGRP):
            sp = grp % 2
            t0 = grp * PREPG
            w = [pq_user[sp]] if pq_user[sp] else []
            ev_ld = S.emit("gpsimd", lambda E, SM, RG, grp=grp, sp=sp: E.dma_start(
                out=PQ[:, 512 * sp:512 * (sp + 1)], in_=gq[grp]),
                waits=w, inc=ld_prep[sp])
            wdeq = [ev_ld, ev_zp]
            if pd_user[sp]:
                wdeq.append(pd_user[sp])
            for i in range(PREPG):
                t = t0 + i
                ev_dq = S.emit("vector", lambda E, SM, RG, i=i, sp=sp, t=t: E.tensor_scalar(
                    PD[:, 512 * sp + 128 * i:512 * sp + 128 * (i + 1)],
                    PQ[:, 512 * sp + 128 * i:512 * sp + 128 * (i + 1)],
                    s_ds[:, t:t + 1], s_dz[:, t:t + 1], ALU.mult, ALU.add),
                    waits=wdeq if i == 0 else [], inc=prep_deq)
            pq_user[sp] = ev_dq
            wtp = [ev_dq, ev_const]
            if grp >= 1:
                wtp.append(S.cur(prep_ev))
            for i in range(PREPG):
                ev_tp = S.emit("tensor", lambda E, SM, RG, i=i, sp=sp: E.transpose(
                    P_BF[:, 128 * i:128 * (i + 1)],
                    PD[:, 512 * sp + 128 * i:512 * sp + 128 * (i + 1)], s_eye[:]),
                    waits=wtp if i == 0 else [], inc=prep_tp)
            pd_user[sp] = ev_tp
            wev = [ev_tp]
            if pn_user[sp]:
                wev.append(S.cur(prep_st))
            ev_ev = S.emit("vector", lambda E, SM, RG, sp=sp: E.tensor_copy(
                PN[:, 512 * sp:512 * (sp + 1)], P_BF[:, 0:512]), waits=wev, inc=prep_ev)
            dst = (wconA[:, 128 * t0:128 * t0 + 512] if t0 < 2 * TPC
                   else wconB[:, 128 * (t0 - 2 * TPC):128 * (t0 - 2 * TPC) + 512])
            ev_st = S.emit("sync", lambda E, SM, RG, dst=dst, sp=sp: E.dma_start(
                out=dst, in_=PN[:, 512 * sp:512 * (sp + 1)]),
                waits=[ev_ev], inc=prep_st)
            pn_user[sp] = ev_st


        ev_wgA = S.emit("gpsimd", lambda E, SM, RG: E.collective_compute(
            "AllGather", ALU.bypass, replica_groups=[[0, 1, 2, 3, 4, 5, 6, 7]],
            ins=[wconA.opt()], outs=[wgatA.opt()]),
            waits=[ev_prepA if False else S.cur(prep_st)], inc=cc_done)


        def tcol(sp, ti, m):
            return 12288 * sp + 3072 * (m % 4) + 1024 * ti + 512 * (m // 4)

        def q_blk(sp, m):
            c0 = tcol(sp, 0, m)
            return A[:, c0:c0 + 512]

        def k_chk(sp, c):
            c0 = tcol(sp, 1, c // 4) + 128 * (c % 4)
            return A[:, c0:c0 + 128]

        def v_chk(sp, c):
            c0 = tcol(sp, 2, c // 4) + 128 * (c % 4)
            return A[:, c0:c0 + 128]

        unit_end = [None, None]
        prep_ev_total = S.cur(prep_ev)
        ld_msk = dm("ld_msk")
        ev_mload = S.emit("gpsimd", lambda E, SM, RG: E.dma_start(
            out=s_msk.rearrange("p (w q) -> p w q", w=4),
            in_=mask_d.rearrange("w p q -> p w q")),
            waits=[S.cur(st_x0)], inc=ld_msk)
        for su in range(8):
            sp = su % 2
            w0 = []
            if unit_end[sp] is not None:
                w0.append(unit_end[sp])
            for jj in range(12):
                r_ = jj % 4
                ti = jj // 4
                gat = q_gat if ti == 0 else kv_gat
                dstc = 12288 * sp + 3072 * r_ + 1024 * ti
                wj = list(w0) + ([ev_agQ] if ti == 0 else [ev_agKV])
                S.emit("gpsimd", lambda E, SM, RG, su=su, jj=jj, ti=ti, r_=r_,
                       dstc=dstc, gat=gat: E.indirect_dma_start(
                    A[:, dstc:dstc + 1024], None, gat,
                    IndirectOffsetOnAxis(s_aidx[:, 12 * su + (r_ if ti == 0 else 4 * ti + r_):
                                                12 * su + (r_ if ti == 0 else 4 * ti + r_) + 1], 0)),
                    waits=wj if jj == 0 else ([ev_agQ] if ti == 0 and r_ == 0 and False else []),
                    inc=ld_slot[sp])
            ev_asm = S.cur(ld_slot[sp])
            for g in range(4):
                wv_ = [ev_asm, S.cur(vtp_ev)] if g == 0 else [S.cur(vtp_ev)]
                if su == 0 and g == 0:
                    wv_.append(prep_ev_total)
                for i in range(8):
                    c = 8 * g + i
                    ev_t = S.emit("tensor", lambda E, SM, RG, c=c, i=i, sp=sp: E.transpose(
                        P_BF[:, 128 * i:128 * (i + 1)], v_chk(sp, c), s_eye[:]),
                        waits=wv_ if i == 0 else [], inc=vtp)
                for i in range(8):
                    c = 8 * g + i
                    S.emit("vector", lambda E, SM, RG, c=c, i=i, sp=sp: E.tensor_copy(
                        v_chk(sp, c), P_BF[:, 128 * i:128 * (i + 1)]),
                        waits=[ev_t] if i == 0 else [], inc=vtp_ev)
            ev_vtp = S.cur(vtp_ev)
            for (ti, bcol) in ((0, 0), (1, 8)):
                for qb in range(4):
                    cb = 12288 * sp + 3072 * qb + 1024 * ti
                    wsq = [ev_asm, S.cur(bnd_mm)]
                    ev_bsq = S.emit("vector", lambda E, SM, RG, cb=cb: E.tensor_mul(
                        s_sq[:, 0:1024], A[:, cb:cb + 1024], A[:, cb:cb + 1024]),
                        waits=wsq, inc=bnd_sq)
                    for g in range(2):
                        ev_bmm = S.emit("tensor", lambda E, SM, RG, g=g: E.matmul(
                            P_AUX[0:1, :], s_oc[:], s_sq[:, 512 * g:512 * (g + 1)],
                            start=True, stop=True),
                            waits=[ev_bsq, S.cur(bnd_red), S.cur(rs_cp)], inc=bnd_mm)
                        ev_bev = S.emit("vector", lambda E, SM, RG: E.tensor_copy(
                            s_msq[0:1, 0:512], P_AUX[0:1, :]),
                            waits=[ev_bmm], inc=bnd_red)
                        S.emit("vector", lambda E, SM, RG, bcol=bcol, qb=qb, g=g: E.reduce_max(
                            s_bnd[0:1, bcol + 2 * qb + g: bcol + 2 * qb + g + 1],
                            s_msq[0:1, 0:512], axis=mybir.AxisListType.X),
                            waits=[ev_bev], inc=bnd_red)
                S.emit("vector", lambda E, SM, RG, bcol=bcol: E.reduce_max(
                    s_bnd[0:1, bcol + 16: bcol + 17], s_bnd[0:1, bcol:bcol + 8],
                    axis=mybir.AxisListType.X), waits=[S.cur(bnd_red)], inc=bnd_red)
            ev_b2 = S.emit("vector", lambda E, SM, RG: E.tensor_mul(
                s_bnd[0:1, 32:33], s_bnd[0:1, 16:17], s_bnd[0:1, 24:25]),
                waits=[S.cur(bnd_red)], inc=bnd_red)
            ev_bl = S.emit("scalar", lambda E, SM, RG: E.activation(
                s_bnd[0:1, 33:34], s_bnd[0:1, 32:33], AF.Ln, bias=s_eps[0:1, 1:2]),
                waits=[ev_b2, ev_eps], inc=bnd_ev)
            S.emit("scalar", lambda E, SM, RG: E.activation(
                s_bnd[0:1, 34:35], s_bnd[0:1, 33:34], AF.Exp, scale=0.5), inc=bnd_ev)
            ev_bs = S.emit("vector", lambda E, SM, RG: E.tensor_scalar_mul(
                s_bnd[0:1, 35:36], s_bnd[0:1, 34:35], -1.02 * SCALE),
                waits=[S.cur(bnd_ev)], inc=bnd_red)
            ev_bb = S.emit("vector", lambda E, SM, RG: E.tensor_copy(
                s_rbf[0:1, 512:513], s_bnd[0:1, 35:36]), waits=[ev_bs], inc=bnd_red)
            ev_bcm = S.emit("tensor", lambda E, SM, RG: E.matmul(
                P_AUX[:, 0:1], s_or[:], s_rbf[0:1, 512:513], start=True, stop=True),
                waits=[ev_bb, S.cur(bnd_ev)], inc=bnd_mm)
            ev_bias = S.emit("scalar", lambda E, SM, RG: E.activation(
                s_bia[:], P_AUX[:, 0:1], AF.Copy), waits=[ev_bcm], inc=bnd_ev)

            gp = {"exp": [None] * 4, "pbu": [None] * 8}
            for m in range(8):
                nch = 4 * (m + 1)
                mpar = m % 2
                for c in range(nch):
                    p4 = c % 4
                    p8 = c % 8
                    wS = [ev_bias]
                    if gp["exp"][p4] is not None:
                        wS.append(gp["exp"][p4])
                    ev_smm = S.emit("tensor", lambda E, SM, RG, c=c, p4=p4,
                                    sp=sp, m=m: E.matmul(
                        P_BIG[:, 512 * p4:512 * (p4 + 1)],
                        k_chk(sp, c), q_blk(sp, m),
                        start=True, stop=True),
                        waits=wS, inc=att_s)
                    wE = [ev_smm]
                    if gp["pbu"][p8] is not None:
                        wE.append(gp["pbu"][p8])
                    ev_exp = S.emit("scalar", lambda E, SM, RG, p8=p8, p4=p4: E.activation(
                        s_pb[:, 512 * p8:512 * (p8 + 1)],
                        P_BIG[:, 512 * p4:512 * (p4 + 1)],
                        AF.Exp, bias=s_bia[:], scale=SCALE),
                        waits=wE, inc=att_e)
                    gp["exp"][p4] = ev_exp
                    ev_mask = None
                    if c >= 4 * m:
                        wvar = c - 4 * m
                        ev_mask = S.emit("vector", lambda E, SM, RG, p8=p8,
                                         wvar=wvar: E.tensor_mul(
                            s_pb[:, 512 * p8:512 * (p8 + 1)],
                            s_pb[:, 512 * p8:512 * (p8 + 1)],
                            s_msk[:, 512 * wvar:512 * (wvar + 1)]),
                            waits=[ev_exp, ev_mload], inc=att_m)
                    wP = [ev_exp if ev_mask is None else ev_mask, ev_vtp]
                    if c == 0:
                        wP.append(S.cur(att_ev))
                        wP.append(S.cur(rs_cp))
                    ev_pv = S.emit("tensor", lambda E, SM, RG, c=c, p8=p8,
                                   sp=sp, mpar=mpar, nch=nch: E.matmul(
                        P_MED[:, 512 * mpar:512 * (mpar + 1)],
                        v_chk(sp, c),
                        s_pb[:, 512 * p8:512 * (p8 + 1)],
                        start=(c == 0), stop=(c == nch - 1)),
                        waits=wP, inc=att_pv)
                    ev_rs = S.emit("tensor", lambda E, SM, RG, c=c, p8=p8,
                                   nch=nch: E.matmul(
                        P_AUX[0:1, 0:512], s_oc[:],
                        s_pb[:, 512 * p8:512 * (p8 + 1)],
                        start=(c == 0), stop=(c == nch - 1)), waits=[], inc=rs_mm)
                    gp["pbu"][p8] = ev_rs
                aop = m % 2
                ev_sc = S.emit("scalar", lambda E, SM, RG: E.activation(
                    s_rows[:, 0:512], P_AUX[0:1, 0:512], AF.Copy),
                    waits=[ev_rs], inc=rs_cp)
                ev_rc = S.emit("vector", lambda E, SM, RG: E.reciprocal(
                    s_rows[0:1, 512:1024], s_rows[0:1, 0:512]),
                    waits=[ev_sc], inc=rs_ve)
                ev_rb = S.emit("vector", lambda E, SM, RG: E.tensor_copy(
                    s_rbf[0:1, 0:512], s_rows[0:1, 512:1024]), inc=rs_ve)
                ev_bcmm = S.emit("tensor", lambda E, SM, RG: E.matmul(
                    P_AUX[:, 0:512], s_or[:], s_rbf[0:1, 0:512],
                    start=True, stop=True), waits=[ev_rb], inc=rs_mm)
                ev_rev = S.emit("scalar", lambda E, SM, RG: E.activation(
                    s_recb[:], P_AUX[:, 0:512], AF.Copy), waits=[ev_bcmm], inc=rs_cp)
                ev_ae = S.emit("vector", lambda E, SM, RG, aop=aop, mpar=mpar: E.tensor_mul(
                    s_ao[:, 512 * aop:512 * (aop + 1)],
                    P_MED[:, 512 * mpar:512 * (mpar + 1)], s_recb[:]),
                    waits=[ev_pv, ev_rev, S.cur(st_ao)], inc=att_ev)
                S.emit("sync", lambda E, SM, RG, su=su, m=m, aop=aop: E.dma_start(
                    out=ao_con[128 * su:128 * (su + 1), 512 * m:512 * (m + 1)],
                    in_=s_ao[:, 512 * aop:512 * (aop + 1)]),
                    waits=[ev_ae], inc=st_ao)
            unit_end[sp] = ev_pv

        ev_ag2 = S.emit("gpsimd", lambda E, SM, RG: E.collective_compute(
            "AllGather", ALU.bypass, replica_groups=[[0, 1, 2, 3, 4, 5, 6, 7]],
            ins=[ao_con.opt()], outs=[ao_gat.opt()]),
            waits=[S.cur(st_ao)], inc=cc_done)
        ev_wgB = S.emit("gpsimd", lambda E, SM, RG: E.collective_compute(
            "AllGather", ALU.bypass, replica_groups=[[0, 1, 2, 3, 4, 5, 6, 7]],
            ins=[wconB.opt()], outs=[wgatB.opt()]),
            waits=[S.cur(prep_st)], inc=cc_done)

        for (gi, regname) in ((0, "coffA"), (1, "coffB")):
            S.emit("sync", lambda E, SM, RG, gi=gi, regname=regname: E.dma_start(
                out=B[:].rearrange("p (h g c) -> p h g c", h=32, g=2)[:, :, gi, :],
                in_=ao_gat[ds(RG["aorow"], 4096), :].rearrange(
                    "(q p) t -> p q t", p=128)[:, :, ds(RG[regname], 512)]),
                waits=[ev_ag2, S.cur(st_x0)] if gi == 0 else [], inc=ld_attn)
        ev_att_all = S.cur(ld_attn)
        o_ev = sm("o_ev")
        oev_events = []
        for n in range(NKC):
            ev_w, sp = load_wblock("wo_t", n)
            qp = n % 2
            mmw = [ev_w, ev_att_all] + proj_psum_waits(qp)
            for k in range(NKC):
                for hf in range(2):
                    ev_mm = S.emit("tensor", lambda E, SM, RG, k=k, hf=hf, sp=sp, qp=qp: E.matmul(
                        P_BIG[:, 1024 * qp + 512 * hf:1024 * qp + 512 * (hf + 1)],
                        w_slot(sp)[:, 128 * k:128 * (k + 1)],
                        B[:, 1024 * k + 512 * hf:1024 * k + 512 * (hf + 1)],
                        start=(k == 0), stop=(k == NKC - 1)),
                        waits=mmw if (k == 0 and hf == 0) else [], inc=o_mm)
            wstate["slot_user"][sp] = ev_mm
            ev_oe = S.emit("scalar", lambda E, SM, RG, n=n, qp=qp: E.activation(
                A[:, 1024 * n:1024 * (n + 1)], P_BIG[:, 1024 * qp:1024 * (qp + 1)],
                AF.Copy), waits=[ev_mm], inc=o_ev)
            pstate["slot_user"][qp] = ev_oe
            oev_events.append(ev_oe)
        ev_bc2 = LN_emit(lambda k: A[:, 1024 * k:1024 * (k + 1)],
                         first_waits=[S.cur(o_ev)])
        for n in range(NKC):
            wx = [ev_x0sp, ev_bc2]
            if n >= 1:
                wx.append(S.cur(res1))
            ev_x0l = S.emit("sync", lambda E, SM, RG, n=n: E.dma_start(
                out=s_x0s[:, 0:1024],
                in_=x0fm_d[128 * n:128 * (n + 1), :]), waits=wx, inc=ld_x0s[0])
            S.emit("vector", lambda E, SM, RG, n=n: E.tensor_mul(
                s_sq[:, 0:1024], A[:, 1024 * n:1024 * (n + 1)], s_rb[:, 0:1024]),
                waits=[ev_bc2, ev_x0l], inc=res1)
            S.emit("vector", lambda E, SM, RG: E.tensor_sub(
                s_sq[:, 0:1024], s_sq[:, 0:1024], s_mb[:, 0:1024]), inc=res1)
            ev_r1 = S.emit("vector", lambda E, SM, RG, n=n: E.tensor_add(
                A[:, 1024 * n:1024 * (n + 1)], s_sq[:, 0:1024],
                s_x0s[:, 0:1024]), inc=res1)
            S.emit("sync", lambda E, SM, RG, n=n: E.dma_start(
                out=x1_d[128 * n:128 * (n + 1), :], in_=A[:, 1024 * n:1024 * (n + 1)]),
                waits=[ev_r1], inc=st_x1)

        ev_bc3 = LN_emit(lambda k: A[:, 1024 * k:1024 * (k + 1)],
                         first_waits=[S.cur(res1)])
        for k in range(NKC):
            ev_ln3 = LN_norm(A[:, 1024 * k:1024 * (k + 1)],
                             B[:, 1024 * k:1024 * (k + 1)], ev_bc3)

        def wgat_dmas(gat, rank_off, tstart, n_tiles, dst, waits, sem):
            done = 0
            evs = None
            first = True
            while done < n_tiles:
                t = tstart + done
                rank, pos = divmod(t, TPC)
                n_here = min(n_tiles - done, TPC - pos)
                evs = S.emit("sync", lambda E, SM, RG, rank=rank, pos=pos, n_here=n_here,
                             done=done, dst=dst, gat=gat, rank_off=rank_off: E.dma_start(
                    out=dst[:, 128 * done:128 * (done + n_here)],
                    in_=gat[128 * rank:128 * (rank + 1),
                            128 * (rank_off + pos):128 * (rank_off + pos + n_here)]),
                    waits=waits if first else [], inc=sem)
                first = False
                done += n_here
            return evs

        f1_slot_user = [None, None]
        f1_gate_user = {}
        f1_up_user = {}
        f1_hst_user = [None]
        for i in range(NIC):
            sp = i % 2
            wv_ = [ev_wgA]
            if i < 2:
                wv_ += [S.cur(norm_done), S.cur(stat_mm), S.cur(st_x1),
                        S.cur(sq_done)]
            if f1_slot_user[sp] is not None:
                wv_.append(f1_slot_user[sp])
            ev_wg = wgat_dmas(wgatA, 0, 32 * i, 32,
                              A[:, 4096 * sp:4096 * (sp + 1)], wv_, ld_fg[sp])
            ev_wu = wgat_dmas(wgatA, TPC, 32 * i, 32,
                              A[:, 8192 + 4096 * sp:8192 + 4096 * (sp + 1)],
                              [], ld_fg[sp])
            gbase = 1024 * (i % 2)
            mmw = [ev_wu, ev_ln3]
            for hf in range(2):
                gu = f1_gate_user.get((i % 2, hf))
                if gu is not None:
                    mmw.append(gu)
            for k in range(NKC):
                for hf in range(2):
                    ev_gmm = S.emit("tensor", lambda E, SM, RG, k=k, hf=hf, sp=sp,
                                    gbase=gbase: E.matmul(
                        P_BIG[:, gbase + 512 * hf:gbase + 512 * (hf + 1)],
                        A[:, 4096 * sp + 128 * k:4096 * sp + 128 * (k + 1)],
                        B[:, 1024 * k + 512 * hf:1024 * k + 512 * (hf + 1)],
                        start=(k == 0), stop=(k == NKC - 1)),
                        waits=mmw if (k == 0 and hf == 0) else [], inc=f1_mm)
            umw = [ev_gmm]
            for hf in range(2):
                uu_ = f1_up_user.get(hf)
                if uu_ is not None:
                    umw.append(uu_)
            for k in range(NKC):
                for hf in range(2):
                    ev_umm = S.emit("tensor", lambda E, SM, RG, k=k, hf=hf, sp=sp: E.matmul(
                        P_MED[:, 512 * hf:512 * (hf + 1)],
                        A[:, 8192 + 4096 * sp + 128 * k:8192 + 4096 * sp + 128 * (k + 1)],
                        B[:, 1024 * k + 512 * hf:1024 * k + 512 * (hf + 1)],
                        start=(k == 0), stop=(k == NKC - 1)),
                        waits=umw if (k == 0 and hf == 0) else [], inc=f1_mm)
            for hf in range(2):
                ev_sil = S.emit("scalar", lambda E, SM, RG, hf=hf, gbase=gbase: E.activation(
                    s_sil[:, 512 * hf:512 * (hf + 1)],
                    P_BIG[:, gbase + 512 * hf:gbase + 512 * (hf + 1)],
                    AF.Silu), waits=[ev_umm] if hf == 0 else [], inc=f1_sil)
                f1_gate_user[(i % 2, hf)] = ev_sil
                wmul = [ev_sil]
                if f1_hst_user[0] is not None and hf == 0:
                    wmul.append(f1_hst_user[0])
                ev_h = S.emit("vector", lambda E, SM, RG, hf=hf: E.tensor_mul(
                    s_hst[:, 512 * hf:512 * (hf + 1)],
                    s_sil[:, 512 * hf:512 * (hf + 1)],
                    P_MED[:, 512 * hf:512 * (hf + 1)]),
                    waits=wmul, inc=f1_mul)
                f1_up_user[hf] = ev_h
            f1_slot_user[sp] = ev_umm
            ev_hdma = S.emit("sync", lambda E, SM, RG, i=i: E.dma_start(
                out=h_d[i], in_=s_hst[:, 0:1024]),
                waits=[ev_h], inc=st_h)
            f1_hst_user[0] = ev_hdma

        def h_ap(i):
            if i < 64:
                return A[:, 512 * i:512 * (i + 1)]
            return B[:, 512 * (i - 64):512 * (i - 64) + 512]

        f2_slot_user = [None, None]
        f2_psum_user = [None, None]
        f2_fst_user = [None, None]
        ev_stat2 = None
        for pas in range(2):
            wh = [S.cur(st_h)]
            if pas == 0:
                wh += [S.cur(att_pv), S.cur(f1_mm), S.cur(norm_done), S.cur(stat_mm)]
            else:
                wh += [S.cur(f2_mm)]
            S.emit("sync", lambda E, SM, RG, pas=pas: E.dma_start(
                out=A[:].rearrange("p (n q) -> p n q", n=64),
                in_=h_d[0:64, :, 512 * pas:512 * (pas + 1)].rearrange("n p q -> p n q")),
                waits=wh, inc=ld_h)
            S.emit("sync", lambda E, SM, RG, pas=pas: E.dma_start(
                out=B[:, 0:11264].rearrange("p (n q) -> p n q", n=22),
                in_=h_d[64:86, :, 512 * pas:512 * (pas + 1)].rearrange("n p q -> p n q")),
                waits=[], inc=ld_h)
            ev_hl = S.cur(ld_h)
            for d in range(NKC):
                sp = d % 2
                wwd = [ev_wgB, ev_hl]
                if f2_slot_user[sp] is not None:
                    wwd.append(f2_slot_user[sp])
                wd_dst = WS[:, 0:11008] if sp == 0 else B[:, 11264:11264 + 11008]
                ev_wd = wgat_dmas(wgatB, 0, NIC * d, NIC, wd_dst, wwd, ld_wd[sp])
                mmw = [ev_wd, ev_hl]
                if f2_psum_user[sp] is not None:
                    mmw.append(f2_psum_user[sp])
                for i in range(NIC):
                    ev_dmm = S.emit("tensor", lambda E, SM, RG, i=i, sp=sp,
                                    wd_dst=wd_dst: E.matmul(
                        P_MED[:, 512 * sp:512 * (sp + 1)],
                        wd_dst[:, 128 * i:128 * (i + 1)],
                        h_ap(i), start=(i == 0), stop=(i == NIC - 1)),
                        waits=mmw if i == 0 else [], inc=f2_mm)
                f2_slot_user[sp] = ev_dmm
                wfe = [ev_dmm, S.cur(st_fo)]
                if ev_stat2 is not None:
                    wfe.append(ev_stat2)
                ev_fe = S.emit("scalar", lambda E, SM, RG, sp=sp: E.activation(
                    s_fst[:, 512 * sp:512 * (sp + 1)],
                    P_MED[:, 512 * sp:512 * (sp + 1)], AF.Copy),
                    waits=wfe, inc=f2_ev)
                f2_psum_user[sp] = ev_fe
                ev_fq = S.emit("scalar", lambda E, SM, RG, sp=sp: E.activation(
                    s_sq[:, 0:512], s_fst[:, 512 * sp:512 * (sp + 1)], AF.Square),
                    inc=f2_ev)
                S.emit("tensor", lambda E, SM, RG, d=d, pas=pas, sp=sp: E.matmul(
                    P_BIG[0:1, 512 * pas:512 * (pas + 1)], s_oc[:],
                    s_fst[:, 512 * sp:512 * (sp + 1)],
                    start=(d == 0), stop=(d == NKC - 1)),
                    waits=[ev_fe], inc=stat_mm)
                ev_stat2 = S.emit("tensor", lambda E, SM, RG, d=d, pas=pas: E.matmul(
                    P_BIG[0:1, 1024 + 512 * pas:1024 + 512 * (pas + 1)], s_oc[:],
                    s_sq[:, 0:512], start=(d == 0), stop=(d == NKC - 1)),
                    waits=[ev_fq], inc=stat_mm)
                ev_fo = S.emit("sync", lambda E, SM, RG, d=d, pas=pas, sp=sp: E.dma_start(
                    out=fo_d[pas, d], in_=s_fst[:, 512 * sp:512 * (sp + 1)]),
                    waits=[ev_fe], inc=st_fo)
                f2_fst_user[sp] = ev_fo
        ev_bc4 = LN_finalize(ev_stat2)

        f3_user = [None, None]
        for d in range(NKC):
            sp = d % 2
            wfx = [S.cur(st_fo), S.cur(st_x1)] if d < 2 else [f3_user[sp]]
            fxb = s_fx3[sp]
            ev_lf = S.emit("sync", lambda E, SM, RG, d=d, fxb=fxb: E.dma_start(
                out=fxb[:, 0:1024].rearrange("p (a q) -> p a q", a=2),
                in_=fo_d[:, d].rearrange("a p q -> p a q")),
                waits=wfx, inc=ld_fx[sp])
            ev_lx = S.emit("sync", lambda E, SM, RG, d=d, fxb=fxb: E.dma_start(
                out=fxb[:, 1024:2048],
                in_=x1_d[128 * d:128 * (d + 1), :]), waits=[], inc=ld_fx[sp])
            wv_ = [S.cur(ld_fx[sp]), ev_bc4, S.cur(st_x2)]
            S.emit("vector", lambda E, SM, RG, fxb=fxb: E.tensor_mul(
                s_sq[:, 0:1024], fxb[:, 0:1024], s_rb[:, 0:1024]),
                waits=wv_, inc=f3_v)
            S.emit("vector", lambda E, SM, RG: E.tensor_sub(
                s_sq[:, 0:1024], s_sq[:, 0:1024], s_mb[:, 0:1024]), inc=f3_v)
            ev_x2 = S.emit("vector", lambda E, SM, RG, fxb=fxb: E.tensor_add(
                s_sq[:, 0:1024], s_sq[:, 0:1024], fxb[:, 1024:2048]), inc=f3_v)
            ev_sx2 = S.emit("sync", lambda E, SM, RG, d=d: E.dma_start(
                out=x2_d[128 * d:128 * (d + 1), :], in_=s_sq[:, 0:1024]),
                waits=[ev_x2], inc=st_x2)
            f3_user[sp] = ev_x2
        out_user = [None, None]
        for tc in range(NT):
            sp = tc % 2
            w = [S.cur(st_x2)]
            if out_user[sp] is not None:
                w.append(S.cur(st_out))
            ev_tl = S.emit("sync", lambda E, SM, RG, tc=tc, sp=sp: E.dma_start_transpose(
                WS[:, 4096 * sp:4096 * (sp + 1)], x2_d[:, 128 * tc:128 * (tc + 1)]),
                waits=w, inc=ld_out[sp])
            ev_so = S.emit("gpsimd", lambda E, SM, RG, tc=tc, sp=sp: E.dma_start(
                out=out_d[128 * tc:128 * (tc + 1), :],
                in_=WS[:, 4096 * sp:4096 * (sp + 1)]),
                waits=[ev_tl], inc=st_out)
            out_user[sp] = ev_so
        S.emit("gpsimd", lambda E, SM, RG: E.nop(), waits=[S.cur(st_out)])

        st2 = ExitStack()
        with st2:
            sems = {name: st2.enter_context(nc.semaphore(name)) for name in S.plan}
            with nc.Block() as block:

                @block.sync
                def _(E):
                    pid = E.partition_id()
                    rho = E.compute_val(pid % 4)
                    bat = E.compute_val(pid // 4)
                    RG = {}
                    RG["coffA"] = E.compute_val(rho * 512)
                    RG["coffB"] = E.compute_val(rho * 512 + 2048)
                    RG["aorow"] = E.compute_val(bat * 4096)
                    S.run("sync", E, sems, RG)

                @block.gpsimd
                def _(E):
                    S.run("gpsimd", E, sems, {})

                @block.tensor
                def _(E):
                    S.run("tensor", E, sems, {})

                @block.vector
                def _(E):
                    S.run("vector", E, sems, {})

                @block.scalar
                def _(E):
                    S.run("scalar", E, sems, {})

    return nc


_NC_CACHE = None


def kernel(**inputs):
    global _NC_CACHE
    in_maps = _host_prep(inputs)
    if _NC_CACHE is None:
        _NC_CACHE = _build()
    res = run_bass_kernel_spmd(_NC_CACHE, in_maps, core_ids=list(range(8)))
    out = np.zeros((BATCH, T, C), dtype=np.float32)
    for c in range(8):
        b, j = c // 4, c % 4
        o = np.asarray(res.results[c]["out"], dtype=np.float32)
        out[b, 512 * j:512 * (j + 1)] = o[0:512]
        out[b, 512 * (j + 4):512 * (j + 5)] = o[512:1024]
    return out


# version 08
# speedup vs baseline: 1.1198x; 1.1198x over previous
import numpy as np
import ml_dtypes
import concourse.bass as bass
import concourse.mybir as mybir
from concourse.bass_utils import run_bass_kernel_spmd
from concourse.bass import IndirectOffsetOnAxis
from contextlib import ExitStack

F32 = mybir.dt.float32
BF16 = mybir.dt.bfloat16
I32 = mybir.dt.int32
AF = mybir.ActivationFunctionType
ALU = mybir.AluOpType
NPBF = ml_dtypes.bfloat16
ds = bass.ds

C = 4096
T = 4096
BATCH = 2
TCORE = 1024
H = 32
NKC = C // 128
NIC = 11008 // 128
NT = TCORE // 128
SCALE = 128.0 ** -0.5
EPS = 1e-5
GT = NKC * NIC
TPC = GT // 8
PREPG = 4
NGRP = 3 * TPC // PREPG
ENGINES = ("tensor", "vector", "scalar", "sync", "gpsimd")



def _pairperm():
    src = np.arange(C).reshape(H, 64, 2).transpose(0, 2, 1)
    return src.reshape(-1)


def _tile_attn_w(w, perm=None):
    if perm is not None:
        w = w[perm]
    wt = np.ascontiguousarray(w.T)
    t = wt.reshape(NKC, 128, NKC, 128).transpose(2, 1, 0, 3)
    return np.ascontiguousarray(t.reshape(NKC, 128, C))


def _tile_q_gateup(q):
    qt = np.ascontiguousarray(q.T)
    t = qt.reshape(NIC, 128, NKC, 128).transpose(0, 2, 1, 3)
    return np.ascontiguousarray(t.reshape(GT, 128, 128))


def _tile_q_down(q):
    qt = np.ascontiguousarray(q.T)
    t = qt.reshape(NKC, 128, NIC, 128).transpose(0, 2, 1, 3)
    return np.ascontiguousarray(t.reshape(GT, 128, 128))


def _sz_gateup(s):
    st = s.T.reshape(NIC, 128, NKC).transpose(0, 2, 1)
    return np.ascontiguousarray(st.reshape(GT, 128))


def _sz_down(s):
    st = s.T.reshape(NKC, 128, NIC).transpose(0, 2, 1)
    return np.ascontiguousarray(st.reshape(GT, 128))


def _host_prep(inp):
    x = np.asarray(inp["L0_attn_x"], dtype=np.float32)
    perm = _pairperm()
    wq_t = _tile_attn_w(np.asarray(inp["L0_attn_wq"], np.float32), perm)
    wk_t = _tile_attn_w(np.asarray(inp["L0_attn_wk"], np.float32), perm)
    wv_t = _tile_attn_w(np.asarray(inp["L0_attn_wv"], np.float32))
    wo_t = _tile_attn_w(np.asarray(inp["L0_attn_wo"], np.float32))

    g_tiles = _tile_q_gateup(np.asarray(inp["L0_ffn_gate_q"], np.int32))
    u_tiles = _tile_q_gateup(np.asarray(inp["L0_ffn_up_q"], np.int32))
    d_tiles = _tile_q_down(np.asarray(inp["L0_ffn_down_q"], np.int32))
    g_s = _sz_gateup(np.asarray(inp["L0_ffn_gate_s"], np.float32))
    g_z = _sz_gateup(np.asarray(inp["L0_ffn_gate_z"], np.float32))
    u_s = _sz_gateup(np.asarray(inp["L0_ffn_up_s"], np.float32))
    u_z = _sz_gateup(np.asarray(inp["L0_ffn_up_z"], np.float32))
    d_s = _sz_down(np.asarray(inp["L0_ffn_down_s"], np.float32))
    d_z = _sz_down(np.asarray(inp["L0_ffn_down_z"], np.float32))

    masks = np.zeros((4, 128, 512), dtype=NPBF)
    uu = np.arange(128)[:, None]
    tt = np.arange(512)[None, :]
    for w in range(4):
        masks[w] = (uu + 128 * w <= tt).astype(NPBF)

    in_maps = []
    for c in range(8):
        b, j = c // 4, c % 4
        blocks = (j, j + 4)
        xs = np.concatenate([x[b, 512 * m:512 * (m + 1)] for m in blocks], axis=0)
        gidx = np.concatenate([np.arange(512 * m, 512 * (m + 1)) for m in blocks])
        cosr = np.cos(gidx.astype(np.float64)).astype(NPBF)
        sinr = np.sin(gidx.astype(np.float64)).astype(NPBF)
        cosb = np.ascontiguousarray(np.broadcast_to(cosr, (128, TCORE)))
        sinb = np.concatenate([np.broadcast_to(-sinr, (64, TCORE)),
                               np.broadcast_to(sinr, (64, TCORE))], axis=0)
        sinb = np.ascontiguousarray(sinb.astype(NPBF))
        sl = slice(TPC * c, TPC * (c + 1))
        qtiles = np.concatenate([g_tiles[sl], u_tiles[sl], d_tiles[sl]], axis=0)
        qtiles = np.ascontiguousarray(
            qtiles.reshape(3 * TPC // PREPG, PREPG, 128, 128)
            .transpose(0, 2, 1, 3).reshape(3 * TPC // PREPG, 128, PREPG * 128))
        dq_s = np.concatenate([g_s[sl], u_s[sl], d_s[sl]], axis=0).T
        dq_z = np.concatenate([g_z[sl], u_z[sl], d_z[sl]], axis=0).T
        asm_idx = np.zeros((128, 96), np.int32)
        for su in range(8):
            for r in range(4):
                base = 1024 * j + 128 * su + np.arange(128)
                asm_idx[:, su * 12 + r] = 4096 * (4 * b + r) + base
                asm_idx[:, su * 12 + 4 + r] = 8192 * (4 * b + r) + base
                asm_idx[:, su * 12 + 8 + r] = 8192 * (4 * b + r) + 4096 + base
        in_maps.append(dict(
            x_slice=np.ascontiguousarray(xs),
            asm_idx=asm_idx,
            wq_t=wq_t, wk_t=wk_t, wv_t=wv_t, wo_t=wo_t,
            gq=np.ascontiguousarray(qtiles),
            deq_s=np.ascontiguousarray(dq_s), deq_z=np.ascontiguousarray(dq_z),
            rope_cos=cosb, rope_sin=sinb, mask01=masks,
            ones_col=np.ones((128, 1), dtype=NPBF),
            ones_row=np.ones((1, 128), dtype=NPBF),
            eye=np.eye(128, dtype=NPBF),
            jmat=np.ascontiguousarray(np.roll(np.eye(128, dtype=NPBF), 64, axis=1)),
        ))
    return in_maps



class Sched:
    def __init__(self):
        self.steps = {e: [] for e in ENGINES}
        self.plan = {}
        self.isdma = {}
        self.owner = {}

    def sem(self, name, dma=False):
        assert name not in self.plan
        self.plan[name] = 0
        self.isdma[name] = dma
        self.owner[name] = None
        return name

    def cur(self, name):
        return (name, self.plan[name])

    def emit(self, eng, fn, waits=(), inc=None):
        ev = None
        if inc is not None:
            amt = 16 if self.isdma[inc] else 1
            if self.owner[inc] is None:
                self.owner[inc] = eng
            assert self.owner[inc] == eng, f"{inc}: {self.owner[inc]} vs {eng}"
            self.plan[inc] += amt
            ev = (inc, self.plan[inc], amt)
        ww = []
        for w in waits:
            if w is None:
                continue
            sn, v = w[0], w[1]
            if self.isdma[sn]:
                assert v == self.plan[sn], f"partial dma wait {sn}: {v} != {self.plan[sn]}"
            ww.append((sn, v))
        self.steps[eng].append((fn, tuple(ww), ev))
        return (ev[0], ev[1]) if ev else None

    def run(self, eng, E, SM, RG):
        lastw = {}
        for idx, (fn, waits, ev) in enumerate(self.steps[eng]):
            for (sn, v) in waits:
                if lastw.get(sn, -1) < v:
                    E.wait_ge(SM[sn], v)
                    lastw[sn] = v
            try:
                inst = fn(E, SM, RG)
            except Exception:
                import sys
                print(f"Sched.run failed at {eng} step {idx}/{len(self.steps[eng])}",
                      file=sys.stderr)
                raise
            if ev is not None:
                sn, _v, amt = ev
                if amt == 16:
                    inst.then_inc(SM[sn], 16)
                else:
                    inst.then_inc(SM[sn])



def _build():
    nc = bass.Bass("TRN2", target_bir_lowering=False)
    x_in = nc.dram_tensor("x_slice", [TCORE, C], F32, kind="ExternalInput").ap()
    w_t = {n: nc.dram_tensor(n, [NKC, 128, C], F32, kind="ExternalInput").ap()
           for n in ("wq_t", "wk_t", "wv_t", "wo_t")}
    gq = nc.dram_tensor("gq", [NGRP, 128, PREPG * 128], I32, kind="ExternalInput").ap()
    asm_idx_d = nc.dram_tensor("asm_idx", [128, 96], I32, kind="ExternalInput").ap()
    deq_s_d = nc.dram_tensor("deq_s", [128, 3 * TPC], F32, kind="ExternalInput").ap()
    deq_z_d = nc.dram_tensor("deq_z", [128, 3 * TPC], F32, kind="ExternalInput").ap()
    cos_d = nc.dram_tensor("rope_cos", [128, TCORE], BF16, kind="ExternalInput").ap()
    sin_d = nc.dram_tensor("rope_sin", [128, TCORE], BF16, kind="ExternalInput").ap()
    mask_d = nc.dram_tensor("mask01", [4, 128, 512], BF16, kind="ExternalInput").ap()
    oc_d = nc.dram_tensor("ones_col", [128, 1], BF16, kind="ExternalInput").ap()
    or_d = nc.dram_tensor("ones_row", [1, 128], BF16, kind="ExternalInput").ap()
    eye_d = nc.dram_tensor("eye", [128, 128], BF16, kind="ExternalInput").ap()
    jmat_d = nc.dram_tensor("jmat", [128, 128], BF16, kind="ExternalInput").ap()
    out_d = nc.dram_tensor("out", [TCORE, C], F32, kind="ExternalOutput").ap()

    xbf_d = nc.dram_tensor("xbf", [TCORE, C], BF16).ap()
    x0fm_d = nc.dram_tensor("x0fm", [C, TCORE], BF16).ap()
    x1_d = nc.dram_tensor("x1d", [C, TCORE], BF16).ap()
    x2_d = nc.dram_tensor("x2d", [C, TCORE], BF16).ap()
    kvq_con = nc.dram_tensor("kvq_con", [3 * C, TCORE], BF16).ap()
    kv_gat = nc.dram_tensor("kv_gat", [8 * 2 * C, TCORE], BF16, addr_space="Shared").ap()
    q_gat = nc.dram_tensor("q_gat", [8 * C, TCORE], BF16, addr_space="Shared").ap()
    wconA = nc.dram_tensor("wconA", [128, 2 * TPC * 128], BF16).ap()
    wgatA = nc.dram_tensor("wgatA", [8 * 128, 2 * TPC * 128], BF16, addr_space="Shared").ap()
    wconB = nc.dram_tensor("wconB", [128, TPC * 128], BF16).ap()
    wgatB = nc.dram_tensor("wgatB", [8 * 128, TPC * 128], BF16, addr_space="Shared").ap()
    ao_con = nc.dram_tensor("ao_con", [8 * 128, T], BF16).ap()
    ao_gat = nc.dram_tensor("ao_gat", [8 * 8 * 128, T], BF16, addr_space="Shared").ap()
    h_d = nc.dram_tensor("h_d", [NIC, 128, 1024], BF16).ap()
    fo_d = nc.dram_tensor("fo_d", [2, NKC, 128, 512], BF16).ap()

    S = Sched()
    sm = lambda n: S.sem(n, dma=False)
    dm = lambda n: S.sem(n, dma=True)

    st = ExitStack()
    with st:
        E_ = st.enter_context
        A = E_(nc.sbuf_tensor("A", [128, 32768], BF16))
        B = E_(nc.sbuf_tensor("Bb", [128, 32768], BF16))
        WS = E_(nc.sbuf_tensor("WS", [128, 11264], BF16))
        PQ = E_(nc.sbuf_tensor("PQ", [128, 1024], I32))
        PD = E_(nc.sbuf_tensor("PD", [128, 1024], BF16))
        PN = E_(nc.sbuf_tensor("PN", [128, 1024], BF16))
        s_ds = E_(nc.sbuf_tensor("s_ds", [128, 3 * TPC], F32))
        s_dz = E_(nc.sbuf_tensor("s_dz", [128, 3 * TPC], F32))
        s_cos = E_(nc.sbuf_tensor("s_cos", [128, TCORE], BF16))
        s_sin = E_(nc.sbuf_tensor("s_sin", [128, TCORE], BF16))
        s_oc = E_(nc.sbuf_tensor("s_oc", [128, 1], BF16))
        s_or = E_(nc.sbuf_tensor("s_or", [1, 128], BF16))
        s_eye = E_(nc.sbuf_tensor("s_eye", [128, 128], BF16))
        s_j = E_(nc.sbuf_tensor("s_j", [128, 128], BF16))
        s_mb = E_(nc.sbuf_tensor("s_mb", [128, 1024], BF16))
        s_rb = E_(nc.sbuf_tensor("s_rb", [128, 1024], BF16))
        s_rows = E_(nc.sbuf_tensor("s_rows", [1, 2048], F32))
        s_msq = E_(nc.sbuf_tensor("s_msq", [1, 1024], F32))
        s_rbf = E_(nc.sbuf_tensor("s_rbf", [1, 1024], BF16))
        s_bnd = E_(nc.sbuf_tensor("s_bnd", [1, 64], F32))
        s_eps = E_(nc.sbuf_tensor("s_eps", [1, 2], F32))
        s_bia = E_(nc.sbuf_tensor("s_bia", [128, 1], F32))
        s_aidx = E_(nc.sbuf_tensor("s_aidx", [128, 96], I32))
        s_qe = E_(nc.sbuf_tensor("s_qe", [128, 2048], BF16))
        s_qr = E_(nc.sbuf_tensor("s_qr", [128, 2048], BF16))
        s_sq = E_(nc.sbuf_tensor("s_sq", [128, 1024], BF16))
        s_tr1 = s_sq
        s_sil = E_(nc.sbuf_tensor("s_sil", [128, 1024], BF16))
        s_hst = E_(nc.sbuf_tensor("s_hst", [128, 1024], BF16))
        s_fst = E_(nc.sbuf_tensor("s_fst", [128, 1024], BF16))
        s_recb = E_(nc.sbuf_tensor("s_recb", [128, 512], BF16))

        s_pb = B[:, 0:4096]
        s_ao = B[:, 4096:5120]
        s_msk = B[:, 5120:7168]
        s_x0s = s_hst
        s_fx3 = (s_qe, s_qr)

        P_BIG = E_(nc.psum_tensor("P_BIG", [128, 2048], F32))
        P_MED = E_(nc.psum_tensor("P_MED", [128, 1024], F32))
        P_BF = E_(nc.psum_tensor("P_BF", [128, 1024], BF16))
        P_AUX = E_(nc.psum_tensor("P_AUX", [128, 512], F32))

        ld_const = dm("ld_const")
        zp_done = sm("zp_done")
        ld_prep = [dm(f"ld_prep{i}") for i in range(2)]
        prep_deq = sm("prep_deq")
        prep_tp = sm("prep_tp")
        prep_ev = sm("prep_ev")
        prep_st = dm("prep_st")
        cc_done = sm("cc_done")
        ld_x = [dm(f"ld_x{i}") for i in range(2)]
        st_xbf = dm("st_xbf")
        ld_xfm = dm("ld_xfm")
        sq_done = sm("sq_done")
        stat_mm = sm("stat_mm")
        ln_fin = sm("ln_fin")
        bc_ev = sm("bc_ev")
        norm_done = sm("norm_done")
        st_x0 = dm("st_x0")
        ld_w = [dm(f"ld_w{i}") for i in range(2)]
        qkv_mm = sm("qkv_mm")
        qkv_ev = sm("qkv_ev")
        rope_done = sm("rope_done")
        st_con = [dm("st_con0"), dm("st_con1")]
        ld_slot = [dm(f"ld_slot{i}") for i in range(2)]
        vtp = sm("vtp")
        vtp_ev = sm("vtp_ev")
        bnd_sq = sm("bnd_sq")
        bnd_mm = sm("bnd_mm")
        bnd_ev = sm("bnd_ev")
        bnd_red = sm("bnd_red")
        att_s = sm("att_s")
        att_e = sm("att_e")
        att_m = sm("att_m")
        att_pv = sm("att_pv")
        att_ev = sm("att_ev")
        rs_mm = sm("rs_mm")
        rs_cp = sm("rs_cp")
        rs_ve = sm("rs_ve")
        st_ao = dm("st_ao")
        ld_attn = dm("ld_attn")
        o_mm = sm("o_mm")
        ld_x0s = [dm("ld_x0s0")]
        res1 = sm("res1")
        st_x1 = dm("st_x1")
        ld_fg = [dm(f"ld_fg{i}") for i in range(2)]
        ld_fu = [dm(f"ld_fu{i}") for i in range(2)]
        f1_mm = sm("f1_mm")
        f1_sil = sm("f1_sil")
        f1_mul = sm("f1_mul")
        st_h = dm("st_h")
        ld_hA = dm("ld_hA")
        ld_hB = dm("ld_hB")
        ld_wd = [dm(f"ld_wd{i}") for i in range(2)]
        f2_mm = sm("f2_mm")
        f2_ev = sm("f2_ev")
        st_fo = dm("st_fo")
        ld_fx = [dm(f"ld_fx{i}") for i in range(2)]
        f3_v = sm("f3_v")
        st_x2 = dm("st_x2")
        ld_out = [dm(f"ld_out{i}") for i in range(2)]
        st_out = dm("st_out")

        for (dst, src) in ((s_ds, deq_s_d), (s_dz, deq_z_d), (s_cos, cos_d),
                           (s_sin, sin_d), (s_oc, oc_d), (s_or, or_d),
                           (s_eye, eye_d), (s_j, jmat_d)):
            S.emit("gpsimd", lambda E, SM, RG, d=dst, s=src: E.dma_start(out=d[:], in_=s),
                   inc=ld_const)
        S.emit("gpsimd", lambda E, SM, RG: E.dma_start(
            out=s_aidx[:], in_=asm_idx_d), inc=ld_const)
        ev_const = S.cur(ld_const)
        ev_zp = S.emit("vector", lambda E, SM, RG: E.scalar_tensor_tensor(
            s_dz[:], s_ds[:], -8.0, s_dz[:], ALU.mult, ALU.add),
            waits=[ev_const], inc=zp_done)
        eps_done = sm("eps_done")
        S.emit("vector", lambda E, SM, RG: E.memset(s_eps[0:1, 0:1], EPS), inc=eps_done)
        ev_eps = S.emit("vector", lambda E, SM, RG: E.memset(s_eps[0:1, 1:2], 1e-12),
                        inc=eps_done)

        ev_xst = None
        for tc in range(NT):
            sp = tc % 2
            w = [S.cur(st_xbf)] if tc >= 2 else []
            ev_l = S.emit("gpsimd", lambda E, SM, RG, tc=tc, sp=sp: E.dma_start(
                out=A[:, 4096 * sp:4096 * (sp + 1)], in_=x_in[128 * tc:128 * (tc + 1), :]),
                waits=w, inc=ld_x[sp])
            ev_xst = S.emit("gpsimd", lambda E, SM, RG, tc=tc, sp=sp: E.dma_start(
                out=xbf_d[128 * tc:128 * (tc + 1), :], in_=A[:, 4096 * sp:4096 * (sp + 1)]),
                waits=[ev_l], inc=st_xbf)
        for k in range(NKC):
            S.emit("scalar", lambda E, SM, RG, k=k: E.dma_start_transpose(
                B[:, 1024 * k:1024 * (k + 1)], xbf_d[:, 128 * k:128 * (k + 1)]),
                waits=[ev_xst], inc=ld_xfm)
        ev_xfm_all = S.cur(ld_xfm)

        def LN_emit(src, wait_chunk=None, first_waits=()):
            prev_s2 = None
            for k in range(NKC):
                wsq = list(first_waits) if k == 0 else []
                if wait_chunk is not None:
                    wsq.append(wait_chunk(k))
                if prev_s2 is not None:
                    wsq.append(prev_s2)
                ev_sq = S.emit("scalar", lambda E, SM, RG, k=k: E.activation(
                    s_sq[:, 0:1024], src(k), AF.Square), waits=wsq, inc=sq_done)
                for hf in range(2):
                    S.emit("tensor", lambda E, SM, RG, k=k, hf=hf: E.matmul(
                        P_BIG[0:1, 512 * hf:512 * (hf + 1)], s_oc[:],
                        src(k)[:, 512 * hf:512 * (hf + 1)],
                        start=(k == 0), stop=(k == NKC - 1)),
                        waits=(wsq + [ev_const]) if hf == 0 else [], inc=stat_mm)
                for hf in range(2):
                    prev_s2 = S.emit("tensor", lambda E, SM, RG, k=k, hf=hf: E.matmul(
                        P_BIG[0:1, 1024 + 512 * hf:1024 + 512 * (hf + 1)], s_oc[:],
                        s_sq[:, 512 * hf:512 * (hf + 1)],
                        start=(k == 0), stop=(k == NKC - 1)),
                        waits=[ev_sq] if hf == 0 else [], inc=stat_mm)
            return LN_finalize(prev_s2)

        def LN_finalize(ev_mm):
            ev_f1 = S.emit("scalar", lambda E, SM, RG: E.activation(
                s_rows[:, 0:1024], P_BIG[0:1, 0:1024], AF.Copy, scale=1.0 / C),
                waits=[ev_mm], inc=bc_ev)
            ev_f2 = S.emit("scalar", lambda E, SM, RG: E.activation(
                s_rows[:, 1024:2048], P_BIG[0:1, 1024:2048], AF.Copy, scale=1.0 / C),
                inc=bc_ev)
            ev_a = S.emit("vector", lambda E, SM, RG: E.tensor_mul(
                s_msq[:], s_rows[0:1, 0:1024], s_rows[0:1, 0:1024]),
                waits=[ev_f1], inc=ln_fin)
            ev_b = S.emit("vector", lambda E, SM, RG: E.tensor_sub(
                s_msq[:], s_rows[0:1, 1024:2048], s_msq[:]),
                waits=[ev_f2, ev_a], inc=ln_fin)
            ev_c = S.emit("scalar", lambda E, SM, RG: E.activation(
                s_msq[:], s_msq[:], AF.Ln, bias=s_eps[0:1, 0:1]),
                waits=[ev_b, ev_eps], inc=bc_ev)
            ev_d = S.emit("scalar", lambda E, SM, RG: E.activation(
                s_msq[:], s_msq[:], AF.Exp, scale=-0.5), inc=bc_ev)
            ev_e = S.emit("vector", lambda E, SM, RG: E.tensor_mul(
                s_rows[0:1, 0:1024], s_rows[0:1, 0:1024], s_msq[:]),
                waits=[ev_d], inc=ln_fin)
            ev_g = S.emit("vector", lambda E, SM, RG: E.tensor_copy(
                s_rbf[0:1, 0:1024], s_msq[0:1, 0:1024]), waits=[ev_d], inc=ln_fin)
            ev_prev = None
            for hf in range(2):
                wv_ = [ev_g]
                if ev_prev is not None:
                    wv_.append(ev_prev)
                ev_bm = S.emit("tensor", lambda E, SM, RG, hf=hf: E.matmul(
                    P_AUX[:], s_or[:], s_rbf[0:1, 512 * hf:512 * (hf + 1)],
                    start=True, stop=True), waits=wv_, inc=stat_mm)
                ev_prev = S.emit("scalar", lambda E, SM, RG, hf=hf: E.activation(
                    s_rb[:, 512 * hf:512 * (hf + 1)], P_AUX[:], AF.Copy),
                    waits=[ev_bm], inc=bc_ev)
            ev_h = S.emit("vector", lambda E, SM, RG: E.tensor_copy(
                s_rbf[0:1, 0:1024], s_rows[0:1, 0:1024]),
                waits=[ev_e, ev_prev], inc=ln_fin)
            for hf in range(2):
                ev_bm = S.emit("tensor", lambda E, SM, RG, hf=hf: E.matmul(
                    P_AUX[:], s_or[:], s_rbf[0:1, 512 * hf:512 * (hf + 1)],
                    start=True, stop=True), waits=[ev_h, ev_prev], inc=stat_mm)
                ev_prev = S.emit("scalar", lambda E, SM, RG, hf=hf: E.activation(
                    s_mb[:, 512 * hf:512 * (hf + 1)], P_AUX[:], AF.Copy),
                    waits=[ev_bm], inc=bc_ev)
            return ev_prev

        def LN_norm(src_ap, dst_ap, ev_bc, extra=()):
            S.emit("vector", lambda E, SM, RG: E.tensor_mul(
                dst_ap, src_ap, s_rb[:, 0:1024]),
                waits=[ev_bc] + list(extra), inc=norm_done)
            return S.emit("vector", lambda E, SM, RG: E.tensor_sub(
                dst_ap, dst_ap, s_mb[:, 0:1024]), inc=norm_done)

        ev_bc1 = LN_emit(lambda k: B[:, 1024 * k:1024 * (k + 1)],
                         first_waits=[ev_xfm_all])
        for k in range(NKC):
            ev_ln1 = LN_norm(B[:, 1024 * k:1024 * (k + 1)],
                             A[:, 1024 * k:1024 * (k + 1)], ev_bc1)
        ev_x0sp = S.emit("scalar", lambda E, SM, RG: E.dma_start(
            out=x0fm_d[:].rearrange("(k p) t -> p k t", p=128),
            in_=B[:].rearrange("p (k t) -> p k t", k=NKC)),
            waits=[ev_ln1], inc=st_x0)

        def w_slot(sp):
            return WS[:, 4096 * sp: 4096 * (sp + 1)]

        wstate = {"count": 0, "slot_user": [None, None]}

        def load_wblock(name, n):
            sp = wstate["count"] % 2
            w = [wstate["slot_user"][sp]] if wstate["slot_user"][sp] else []
            ev = S.emit("gpsimd", lambda E, SM, RG, name=name, n=n, sp=sp: E.dma_start(
                out=w_slot(sp), in_=w_t[name][n]),
                waits=w, inc=ld_w[sp])
            wstate["count"] += 1
            return ev, sp

        pstate = {"slot_user": [None, None]}

        def proj_psum_waits(qp):
            return [pstate["slot_user"][qp]] if pstate["slot_user"][qp] else []

        gidx = [0]
        qkvs = {"qe_user": [None, None], "qr_user": [None, None],
                "jpsum_user": [None, None], "pending": None}

        def qkv_one(name, n, crow, rope):
            ev_w, sp = load_wblock(name, n)
            qp = gidx[0] % 2
            gidx[0] += 1
            mmw = [ev_w, ev_ln1] + proj_psum_waits(qp)
            for k in range(NKC):
                if k == 8 and qkvs["pending"] is not None:
                    qkvs["pending"]()
                    qkvs["pending"] = None
                for hf in range(2):
                    ev_mm = S.emit("tensor", lambda E, SM, RG, k=k, hf=hf, sp=sp, qp=qp: E.matmul(
                        P_BIG[:, 1024 * qp + 512 * hf:1024 * qp + 512 * (hf + 1)],
                        w_slot(sp)[:, 128 * k:128 * (k + 1)],
                        A[:, 1024 * k + 512 * hf:1024 * k + 512 * (hf + 1)],
                        start=(k == 0), stop=(k == NKC - 1)),
                        waits=mmw if (k == 0 and hf == 0) else [], inc=qkv_mm)
            wstate["slot_user"][sp] = ev_mm
            evw = [ev_mm, S.cur(st_con[qp])]
            if qkvs["qe_user"][qp] is not None:
                evw.append(qkvs["qe_user"][qp])
            ev_ev = S.emit("scalar", lambda E, SM, RG, qp=qp: E.activation(
                s_qe[:, 1024 * qp:1024 * (qp + 1)], P_BIG[:, 1024 * qp:1024 * (qp + 1)],
                AF.Copy), waits=evw, inc=qkv_ev)
            pstate["slot_user"][qp] = ev_ev
            if rope:
                def emit_jmm(qp=qp, ev_ev=ev_ev):
                    wj = [ev_ev, ev_const]
                    ju = qkvs["jpsum_user"][0]
                    if ju is not None:
                        wj.append(ju)
                    for hf in range(2):
                        ev_jmm = S.emit("tensor", lambda E, SM, RG, qp=qp, hf=hf: E.matmul(
                            P_MED[:, 512 * hf:512 * (hf + 1)], s_j[:],
                            s_qe[:, 1024 * qp + 512 * hf:1024 * qp + 512 * (hf + 1)],
                            start=True, stop=True),
                            waits=wj if hf == 0 else [], inc=qkv_mm)
                    wr = [ev_ev, S.cur(st_con[qp])]
                    S.emit("vector", lambda E, SM, RG, qp=qp: E.tensor_mul(
                        s_tr1[:, :], s_qe[:, 1024 * qp:1024 * (qp + 1)], s_cos[:]),
                        waits=wr, inc=rope_done)
                    ev_j2 = S.emit("vector", lambda E, SM, RG, qp=qp: E.tensor_mul(
                        s_qr[:, 1024 * qp:1024 * (qp + 1)], P_MED[:, 0:1024], s_sin[:]),
                        waits=[ev_jmm], inc=rope_done)
                    qkvs["jpsum_user"][0] = ev_j2
                    ev_src = S.emit("vector", lambda E, SM, RG, qp=qp: E.tensor_add(
                        s_qr[:, 1024 * qp:1024 * (qp + 1)],
                        s_qr[:, 1024 * qp:1024 * (qp + 1)], s_tr1[:, :]),
                        inc=rope_done)
                    qkvs["qe_user"][qp] = ev_src
                    ev_st = S.emit("sync", lambda E, SM, RG, crow=crow, n=n, qp=qp: E.dma_start(
                        out=kvq_con[crow + 128 * n: crow + 128 * (n + 1), :],
                        in_=s_qr[:, 1024 * qp:1024 * (qp + 1)]),
                        waits=[ev_src], inc=st_con[qp])
                qkvs["pending"] = emit_jmm
            else:
                ev_st = S.emit("sync", lambda E, SM, RG, crow=crow, n=n, qp=qp: E.dma_start(
                    out=kvq_con[crow + 128 * n: crow + 128 * (n + 1), :],
                    in_=s_qe[:, 1024 * qp:1024 * (qp + 1)]),
                    waits=[ev_ev], inc=st_con[qp])
                qkvs["qe_user"][qp] = None

        for n in range(NKC):
            qkv_one("wk_t", n, C, True)
        for n in range(NKC):
            qkv_one("wv_t", n, 2 * C, False)
        if qkvs["pending"] is not None:
            qkvs["pending"]()
            qkvs["pending"] = None
        ev_agKV = S.emit("gpsimd", lambda E, SM, RG: E.collective_compute(
            "AllGather", ALU.bypass, replica_groups=[[0, 1, 2, 3, 4, 5, 6, 7]],
            ins=[kvq_con[C:3 * C, :].opt()], outs=[kv_gat.opt()]),
            waits=[S.cur(st_con[0]), S.cur(st_con[1])], inc=cc_done)
        for n in range(NKC):
            qkv_one("wq_t", n, 0, True)
        if qkvs["pending"] is not None:
            qkvs["pending"]()
            qkvs["pending"] = None
        ev_agQ = S.emit("gpsimd", lambda E, SM, RG: E.collective_compute(
            "AllGather", ALU.bypass, replica_groups=[[0, 1, 2, 3, 4, 5, 6, 7]],
            ins=[kvq_con[0:C, :].opt()], outs=[q_gat.opt()]),
            waits=[S.cur(st_con[0]), S.cur(st_con[1])], inc=cc_done)
        ev_agA = ev_agQ
        pq_user = [None, None]
        pd_user = [None, None]
        pn_user = [None, None]
        for grp in range(NGRP):
            sp = grp % 2
            t0 = grp * PREPG
            w = [pq_user[sp]] if pq_user[sp] else []
            ev_ld = S.emit("gpsimd", lambda E, SM, RG, grp=grp, sp=sp: E.dma_start(
                out=PQ[:, 512 * sp:512 * (sp + 1)], in_=gq[grp]),
                waits=w, inc=ld_prep[sp])
            wdeq = [ev_ld, ev_zp]
            if pd_user[sp]:
                wdeq.append(pd_user[sp])
            for i in range(PREPG):
                t = t0 + i
                ev_dq = S.emit("vector", lambda E, SM, RG, i=i, sp=sp, t=t: E.tensor_scalar(
                    PD[:, 512 * sp + 128 * i:512 * sp + 128 * (i + 1)],
                    PQ[:, 512 * sp + 128 * i:512 * sp + 128 * (i + 1)],
                    s_ds[:, t:t + 1], s_dz[:, t:t + 1], ALU.mult, ALU.add),
                    waits=wdeq if i == 0 else [], inc=prep_deq)
            pq_user[sp] = ev_dq
            wtp = [ev_dq, ev_const]
            if grp >= 1:
                wtp.append(S.cur(prep_ev))
            for i in range(PREPG):
                ev_tp = S.emit("tensor", lambda E, SM, RG, i=i, sp=sp: E.transpose(
                    P_BF[:, 128 * i:128 * (i + 1)],
                    PD[:, 512 * sp + 128 * i:512 * sp + 128 * (i + 1)], s_eye[:]),
                    waits=wtp if i == 0 else [], inc=prep_tp)
            pd_user[sp] = ev_tp
            wev = [ev_tp]
            if pn_user[sp]:
                wev.append(S.cur(prep_st))
            ev_ev = S.emit("vector", lambda E, SM, RG, sp=sp: E.tensor_copy(
                PN[:, 512 * sp:512 * (sp + 1)], P_BF[:, 0:512]), waits=wev, inc=prep_ev)
            dst = (wconA[:, 128 * t0:128 * t0 + 512] if t0 < 2 * TPC
                   else wconB[:, 128 * (t0 - 2 * TPC):128 * (t0 - 2 * TPC) + 512])
            ev_st = S.emit("sync", lambda E, SM, RG, dst=dst, sp=sp: E.dma_start(
                out=dst, in_=PN[:, 512 * sp:512 * (sp + 1)]),
                waits=[ev_ev], inc=prep_st)
            pn_user[sp] = ev_st


        ev_wgA = S.emit("gpsimd", lambda E, SM, RG: E.collective_compute(
            "AllGather", ALU.bypass, replica_groups=[[0, 1, 2, 3, 4, 5, 6, 7]],
            ins=[wconA.opt()], outs=[wgatA.opt()]),
            waits=[ev_prepA if False else S.cur(prep_st)], inc=cc_done)


        def tcol(sp, ti, m):
            return 12288 * sp + 3072 * (m % 4) + 1024 * ti + 512 * (m // 4)

        def q_blk(sp, m):
            c0 = tcol(sp, 0, m)
            return A[:, c0:c0 + 512]

        def k_chk(sp, c):
            c0 = tcol(sp, 1, c // 4) + 128 * (c % 4)
            return A[:, c0:c0 + 128]

        def v_chk(sp, c):
            c0 = tcol(sp, 2, c // 4) + 128 * (c % 4)
            return A[:, c0:c0 + 128]

        unit_end = [None, None]
        prep_ev_total = S.cur(prep_ev)
        ld_msk = dm("ld_msk")
        ev_mload = S.emit("gpsimd", lambda E, SM, RG: E.dma_start(
            out=s_msk.rearrange("p (w q) -> p w q", w=4),
            in_=mask_d.rearrange("w p q -> p w q")),
            waits=[S.cur(st_x0)], inc=ld_msk)
        for su in range(8):
            sp = su % 2
            w0 = []
            if unit_end[sp] is not None:
                w0.append(unit_end[sp])
            for jj in range(12):
                r_ = jj % 4
                ti = jj // 4
                gat = q_gat if ti == 0 else kv_gat
                dstc = 12288 * sp + 3072 * r_ + 1024 * ti
                wj = list(w0) + ([ev_agQ] if ti == 0 else [ev_agKV])
                S.emit("gpsimd", lambda E, SM, RG, su=su, jj=jj, ti=ti, r_=r_,
                       dstc=dstc, gat=gat: E.indirect_dma_start(
                    A[:, dstc:dstc + 1024], None, gat,
                    IndirectOffsetOnAxis(s_aidx[:, 12 * su + (r_ if ti == 0 else 4 * ti + r_):
                                                12 * su + (r_ if ti == 0 else 4 * ti + r_) + 1], 0)),
                    waits=wj if jj == 0 else ([ev_agQ] if ti == 0 and r_ == 0 and False else []),
                    inc=ld_slot[sp])
            ev_asm = S.cur(ld_slot[sp])
            for g in range(4):
                wv_ = [ev_asm, S.cur(vtp_ev)] if g == 0 else [S.cur(vtp_ev)]
                if su == 0 and g == 0:
                    wv_.append(prep_ev_total)
                for i in range(8):
                    c = 8 * g + i
                    ev_t = S.emit("tensor", lambda E, SM, RG, c=c, i=i, sp=sp: E.transpose(
                        P_BF[:, 128 * i:128 * (i + 1)], v_chk(sp, c), s_eye[:]),
                        waits=wv_ if i == 0 else [], inc=vtp)
                for i in range(8):
                    c = 8 * g + i
                    S.emit("vector", lambda E, SM, RG, c=c, i=i, sp=sp: E.tensor_copy(
                        v_chk(sp, c), P_BF[:, 128 * i:128 * (i + 1)]),
                        waits=[ev_t] if i == 0 else [], inc=vtp_ev)
            ev_vtp = S.cur(vtp_ev)
            for (ti, bcol) in ((0, 0), (1, 8)):
                for qb in range(4):
                    cb = 12288 * sp + 3072 * qb + 1024 * ti
                    wsq = [ev_asm, S.cur(bnd_mm)]
                    ev_bsq = S.emit("vector", lambda E, SM, RG, cb=cb: E.tensor_mul(
                        s_sq[:, 0:1024], A[:, cb:cb + 1024], A[:, cb:cb + 1024]),
                        waits=wsq, inc=bnd_sq)
                    for g in range(2):
                        ev_bmm = S.emit("tensor", lambda E, SM, RG, g=g: E.matmul(
                            P_AUX[0:1, :], s_oc[:], s_sq[:, 512 * g:512 * (g + 1)],
                            start=True, stop=True),
                            waits=[ev_bsq, S.cur(bnd_red), S.cur(rs_cp)], inc=bnd_mm)
                        ev_bev = S.emit("vector", lambda E, SM, RG: E.tensor_copy(
                            s_msq[0:1, 0:512], P_AUX[0:1, :]),
                            waits=[ev_bmm], inc=bnd_red)
                        S.emit("vector", lambda E, SM, RG, bcol=bcol, qb=qb, g=g: E.reduce_max(
                            s_bnd[0:1, bcol + 2 * qb + g: bcol + 2 * qb + g + 1],
                            s_msq[0:1, 0:512], axis=mybir.AxisListType.X),
                            waits=[ev_bev], inc=bnd_red)
                S.emit("vector", lambda E, SM, RG, bcol=bcol: E.reduce_max(
                    s_bnd[0:1, bcol + 16: bcol + 17], s_bnd[0:1, bcol:bcol + 8],
                    axis=mybir.AxisListType.X), waits=[S.cur(bnd_red)], inc=bnd_red)
            ev_b2 = S.emit("vector", lambda E, SM, RG: E.tensor_mul(
                s_bnd[0:1, 32:33], s_bnd[0:1, 16:17], s_bnd[0:1, 24:25]),
                waits=[S.cur(bnd_red)], inc=bnd_red)
            ev_bl = S.emit("scalar", lambda E, SM, RG: E.activation(
                s_bnd[0:1, 33:34], s_bnd[0:1, 32:33], AF.Ln, bias=s_eps[0:1, 1:2]),
                waits=[ev_b2, ev_eps], inc=bnd_ev)
            S.emit("scalar", lambda E, SM, RG: E.activation(
                s_bnd[0:1, 34:35], s_bnd[0:1, 33:34], AF.Exp, scale=0.5), inc=bnd_ev)
            ev_bs = S.emit("vector", lambda E, SM, RG: E.tensor_scalar_mul(
                s_bnd[0:1, 35:36], s_bnd[0:1, 34:35], -1.02 * SCALE),
                waits=[S.cur(bnd_ev)], inc=bnd_red)
            ev_bb = S.emit("vector", lambda E, SM, RG: E.tensor_copy(
                s_rbf[0:1, 512:513], s_bnd[0:1, 35:36]), waits=[ev_bs], inc=bnd_red)
            ev_bcm = S.emit("tensor", lambda E, SM, RG: E.matmul(
                P_AUX[:, 0:1], s_or[:], s_rbf[0:1, 512:513], start=True, stop=True),
                waits=[ev_bb, S.cur(bnd_ev)], inc=bnd_mm)
            ev_bias = S.emit("scalar", lambda E, SM, RG: E.activation(
                s_bia[:], P_AUX[:, 0:1], AF.Copy), waits=[ev_bcm], inc=bnd_ev)

            gp = {"exp": [None] * 4, "pbu": [None] * 8}
            for m in range(8):
                nch = 4 * (m + 1)
                mpar = m % 2
                for c in range(nch):
                    p4 = c % 4
                    p8 = c % 8
                    wS = [ev_bias]
                    if gp["exp"][p4] is not None:
                        wS.append(gp["exp"][p4])
                    ev_smm = S.emit("tensor", lambda E, SM, RG, c=c, p4=p4,
                                    sp=sp, m=m: E.matmul(
                        P_BIG[:, 512 * p4:512 * (p4 + 1)],
                        k_chk(sp, c), q_blk(sp, m),
                        start=True, stop=True),
                        waits=wS, inc=att_s)
                    wE = [ev_smm]
                    if gp["pbu"][p8] is not None:
                        wE.append(gp["pbu"][p8])
                    ev_exp = S.emit("scalar", lambda E, SM, RG, p8=p8, p4=p4: E.activation(
                        s_pb[:, 512 * p8:512 * (p8 + 1)],
                        P_BIG[:, 512 * p4:512 * (p4 + 1)],
                        AF.Exp, bias=s_bia[:], scale=SCALE),
                        waits=wE, inc=att_e)
                    gp["exp"][p4] = ev_exp
                    ev_mask = None
                    if c >= 4 * m:
                        wvar = c - 4 * m
                        ev_mask = S.emit("vector", lambda E, SM, RG, p8=p8,
                                         wvar=wvar: E.tensor_mul(
                            s_pb[:, 512 * p8:512 * (p8 + 1)],
                            s_pb[:, 512 * p8:512 * (p8 + 1)],
                            s_msk[:, 512 * wvar:512 * (wvar + 1)]),
                            waits=[ev_exp, ev_mload], inc=att_m)
                    wP = [ev_exp if ev_mask is None else ev_mask, ev_vtp]
                    if c == 0:
                        wP.append(S.cur(att_ev))
                        wP.append(S.cur(rs_cp))
                    ev_pv = S.emit("tensor", lambda E, SM, RG, c=c, p8=p8,
                                   sp=sp, mpar=mpar, nch=nch: E.matmul(
                        P_MED[:, 512 * mpar:512 * (mpar + 1)],
                        v_chk(sp, c),
                        s_pb[:, 512 * p8:512 * (p8 + 1)],
                        start=(c == 0), stop=(c == nch - 1)),
                        waits=wP, inc=att_pv)
                    ev_rs = S.emit("tensor", lambda E, SM, RG, c=c, p8=p8,
                                   nch=nch: E.matmul(
                        P_AUX[0:1, 0:512], s_oc[:],
                        s_pb[:, 512 * p8:512 * (p8 + 1)],
                        start=(c == 0), stop=(c == nch - 1)), waits=[], inc=rs_mm)
                    gp["pbu"][p8] = ev_rs
                aop = m % 2
                ev_sc = S.emit("scalar", lambda E, SM, RG: E.activation(
                    s_rows[:, 0:512], P_AUX[0:1, 0:512], AF.Copy),
                    waits=[ev_rs], inc=rs_cp)
                ev_rc = S.emit("vector", lambda E, SM, RG: E.reciprocal(
                    s_rows[0:1, 512:1024], s_rows[0:1, 0:512]),
                    waits=[ev_sc], inc=rs_ve)
                ev_rb = S.emit("vector", lambda E, SM, RG: E.tensor_copy(
                    s_rbf[0:1, 0:512], s_rows[0:1, 512:1024]), inc=rs_ve)
                ev_bcmm = S.emit("tensor", lambda E, SM, RG: E.matmul(
                    P_AUX[:, 0:512], s_or[:], s_rbf[0:1, 0:512],
                    start=True, stop=True), waits=[ev_rb], inc=rs_mm)
                ev_rev = S.emit("scalar", lambda E, SM, RG: E.activation(
                    s_recb[:], P_AUX[:, 0:512], AF.Copy), waits=[ev_bcmm], inc=rs_cp)
                ev_ae = S.emit("vector", lambda E, SM, RG, aop=aop, mpar=mpar: E.tensor_mul(
                    s_ao[:, 512 * aop:512 * (aop + 1)],
                    P_MED[:, 512 * mpar:512 * (mpar + 1)], s_recb[:]),
                    waits=[ev_pv, ev_rev, S.cur(st_ao)], inc=att_ev)
                S.emit("sync", lambda E, SM, RG, su=su, m=m, aop=aop: E.dma_start(
                    out=ao_con[128 * su:128 * (su + 1), 512 * m:512 * (m + 1)],
                    in_=s_ao[:, 512 * aop:512 * (aop + 1)]),
                    waits=[ev_ae], inc=st_ao)
            unit_end[sp] = ev_pv

        ev_ag2 = S.emit("gpsimd", lambda E, SM, RG: E.collective_compute(
            "AllGather", ALU.bypass, replica_groups=[[0, 1, 2, 3, 4, 5, 6, 7]],
            ins=[ao_con.opt()], outs=[ao_gat.opt()]),
            waits=[S.cur(st_ao)], inc=cc_done)
        ev_wgB = S.emit("gpsimd", lambda E, SM, RG: E.collective_compute(
            "AllGather", ALU.bypass, replica_groups=[[0, 1, 2, 3, 4, 5, 6, 7]],
            ins=[wconB.opt()], outs=[wgatB.opt()]),
            waits=[S.cur(prep_st)], inc=cc_done)

        for (gi, regname) in ((0, "coffA"), (1, "coffB")):
            S.emit("sync", lambda E, SM, RG, gi=gi, regname=regname: E.dma_start(
                out=B[:].rearrange("p (h g c) -> p h g c", h=32, g=2)[:, :, gi, :],
                in_=ao_gat[ds(RG["aorow"], 4096), :].rearrange(
                    "(q p) t -> p q t", p=128)[:, :, ds(RG[regname], 512)]),
                waits=[ev_ag2, S.cur(st_x0)] if gi == 0 else [], inc=ld_attn)
        ev_att_all = S.cur(ld_attn)
        o_ev = sm("o_ev")
        oev_events = []
        for n in range(NKC):
            ev_w, sp = load_wblock("wo_t", n)
            qp = n % 2
            mmw = [ev_w, ev_att_all] + proj_psum_waits(qp)
            for k in range(NKC):
                for hf in range(2):
                    ev_mm = S.emit("tensor", lambda E, SM, RG, k=k, hf=hf, sp=sp, qp=qp: E.matmul(
                        P_BIG[:, 1024 * qp + 512 * hf:1024 * qp + 512 * (hf + 1)],
                        w_slot(sp)[:, 128 * k:128 * (k + 1)],
                        B[:, 1024 * k + 512 * hf:1024 * k + 512 * (hf + 1)],
                        start=(k == 0), stop=(k == NKC - 1)),
                        waits=mmw if (k == 0 and hf == 0) else [], inc=o_mm)
            wstate["slot_user"][sp] = ev_mm
            ev_oe = S.emit("scalar", lambda E, SM, RG, n=n, qp=qp: E.activation(
                A[:, 1024 * n:1024 * (n + 1)], P_BIG[:, 1024 * qp:1024 * (qp + 1)],
                AF.Copy), waits=[ev_mm], inc=o_ev)
            pstate["slot_user"][qp] = ev_oe
            oev_events.append(ev_oe)
        ev_bc2 = LN_emit(lambda k: A[:, 1024 * k:1024 * (k + 1)],
                         first_waits=[S.cur(o_ev)])
        for n in range(NKC):
            wx = [ev_x0sp, ev_bc2]
            if n >= 1:
                wx.append(S.cur(res1))
            ev_x0l = S.emit("sync", lambda E, SM, RG, n=n: E.dma_start(
                out=s_x0s[:, 0:1024],
                in_=x0fm_d[128 * n:128 * (n + 1), :]), waits=wx, inc=ld_x0s[0])
            S.emit("vector", lambda E, SM, RG, n=n: E.tensor_mul(
                s_sq[:, 0:1024], A[:, 1024 * n:1024 * (n + 1)], s_rb[:, 0:1024]),
                waits=[ev_bc2, ev_x0l], inc=res1)
            S.emit("vector", lambda E, SM, RG: E.tensor_sub(
                s_sq[:, 0:1024], s_sq[:, 0:1024], s_mb[:, 0:1024]), inc=res1)
            ev_r1 = S.emit("vector", lambda E, SM, RG, n=n: E.tensor_add(
                A[:, 1024 * n:1024 * (n + 1)], s_sq[:, 0:1024],
                s_x0s[:, 0:1024]), inc=res1)
            S.emit("sync", lambda E, SM, RG, n=n: E.dma_start(
                out=x1_d[128 * n:128 * (n + 1), :], in_=A[:, 1024 * n:1024 * (n + 1)]),
                waits=[ev_r1], inc=st_x1)

        ev_bc3 = LN_emit(lambda k: A[:, 1024 * k:1024 * (k + 1)],
                         first_waits=[S.cur(res1)])
        for k in range(NKC):
            ev_ln3 = LN_norm(A[:, 1024 * k:1024 * (k + 1)],
                             B[:, 1024 * k:1024 * (k + 1)], ev_bc3)

        def wgat_dmas(gat, rank_off, tstart, n_tiles, dst, waits, sem):
            done = 0
            evs = None
            first = True
            while done < n_tiles:
                t = tstart + done
                rank, pos = divmod(t, TPC)
                n_here = min(n_tiles - done, TPC - pos)
                evs = S.emit("sync", lambda E, SM, RG, rank=rank, pos=pos, n_here=n_here,
                             done=done, dst=dst, gat=gat, rank_off=rank_off: E.dma_start(
                    out=dst[:, 128 * done:128 * (done + n_here)],
                    in_=gat[128 * rank:128 * (rank + 1),
                            128 * (rank_off + pos):128 * (rank_off + pos + n_here)]),
                    waits=waits if first else [], inc=sem)
                first = False
                done += n_here
            return evs

        f1_slot_user = [None, None]
        f1_gate_user = {}
        f1_up_user = {}
        f1_hst_user = [None]

        def f1_loads(i):
            sp = i % 2
            wv_ = [ev_wgA]
            if i < 2:
                wv_ += [S.cur(norm_done), S.cur(stat_mm), S.cur(st_x1),
                        S.cur(sq_done)]
            if f1_slot_user[sp] is not None:
                wv_.append(f1_slot_user[sp])
            ev_wg = wgat_dmas(wgatA, 0, 32 * i, 32,
                              A[:, 4096 * sp:4096 * (sp + 1)], wv_, ld_fg[sp])
            ev_wu = wgat_dmas(wgatA, TPC, 32 * i, 32,
                              A[:, 8192 + 4096 * sp:8192 + 4096 * (sp + 1)],
                              [], ld_fu[sp])
            return ev_wg, ev_wu

        f1_pend = {0: f1_loads(0), 1: f1_loads(1)}
        for i in range(NIC):
            sp = i % 2
            ev_wg, ev_wu = f1_pend.pop(i)
            gbase = 1024 * (i % 2)
            mmw = [ev_wg, ev_ln3]
            for hf in range(2):
                gu = f1_gate_user.get((i % 2, hf))
                if gu is not None:
                    mmw.append(gu)
            for k in range(NKC):
                for hf in range(2):
                    ev_gmm = S.emit("tensor", lambda E, SM, RG, k=k, hf=hf, sp=sp,
                                    gbase=gbase: E.matmul(
                        P_BIG[:, gbase + 512 * hf:gbase + 512 * (hf + 1)],
                        A[:, 4096 * sp + 128 * k:4096 * sp + 128 * (k + 1)],
                        B[:, 1024 * k + 512 * hf:1024 * k + 512 * (hf + 1)],
                        start=(k == 0), stop=(k == NKC - 1)),
                        waits=mmw if (k == 0 and hf == 0) else [], inc=f1_mm)
            umw = [ev_gmm, ev_wu]
            for hf in range(2):
                uu_ = f1_up_user.get(hf)
                if uu_ is not None:
                    umw.append(uu_)
            for k in range(NKC):
                for hf in range(2):
                    ev_umm = S.emit("tensor", lambda E, SM, RG, k=k, hf=hf, sp=sp: E.matmul(
                        P_MED[:, 512 * hf:512 * (hf + 1)],
                        A[:, 8192 + 4096 * sp + 128 * k:8192 + 4096 * sp + 128 * (k + 1)],
                        B[:, 1024 * k + 512 * hf:1024 * k + 512 * (hf + 1)],
                        start=(k == 0), stop=(k == NKC - 1)),
                        waits=umw if (k == 0 and hf == 0) else [], inc=f1_mm)
            for hf in range(2):
                ev_sil = S.emit("scalar", lambda E, SM, RG, hf=hf, gbase=gbase: E.activation(
                    s_sil[:, 512 * hf:512 * (hf + 1)],
                    P_BIG[:, gbase + 512 * hf:gbase + 512 * (hf + 1)],
                    AF.Silu), waits=[ev_umm] if hf == 0 else [], inc=f1_sil)
                f1_gate_user[(i % 2, hf)] = ev_sil
                wmul = [ev_sil]
                if f1_hst_user[0] is not None and hf == 0:
                    wmul.append(f1_hst_user[0])
                ev_h = S.emit("vector", lambda E, SM, RG, hf=hf: E.tensor_mul(
                    s_hst[:, 512 * hf:512 * (hf + 1)],
                    s_sil[:, 512 * hf:512 * (hf + 1)],
                    P_MED[:, 512 * hf:512 * (hf + 1)]),
                    waits=wmul, inc=f1_mul)
                f1_up_user[hf] = ev_h
            f1_slot_user[sp] = ev_umm
            if i + 2 < NIC:
                f1_pend[i + 2] = f1_loads(i + 2)
            ev_hdma = S.emit("gpsimd", lambda E, SM, RG, i=i: E.dma_start(
                out=h_d[i], in_=s_hst[:, 0:1024]),
                waits=[ev_h], inc=st_h)
            f1_hst_user[0] = ev_hdma

        def h_ap(i):
            if i < 64:
                return A[:, 512 * i:512 * (i + 1)]
            return B[:, 512 * (i - 64):512 * (i - 64) + 512]

        f2_slot_user = [None, None]
        f2_psum_user = [None, None]
        f2_fst_user = [None, None]
        ev_stat2 = None
        hl_evA = [None, None]
        hl_evB = [None, None]

        def f2_hloads(pas):
            wh = [S.cur(st_h)]
            if pas == 0:
                wh += [S.cur(att_pv), S.cur(f1_mm), S.cur(norm_done), S.cur(stat_mm)]
            else:
                wh += [S.cur(f2_mm)]
            hl_evA[pas] = S.emit("gpsimd", lambda E, SM, RG, pas=pas: E.dma_start(
                out=A[:].rearrange("p (n q) -> p n q", n=64),
                in_=h_d[0:64, :, 512 * pas:512 * (pas + 1)].rearrange("n p q -> p n q")),
                waits=wh, inc=ld_hA)
            hl_evB[pas] = S.emit("gpsimd", lambda E, SM, RG, pas=pas: E.dma_start(
                out=B[:, 0:11264].rearrange("p (n q) -> p n q", n=22),
                in_=h_d[64:86, :, 512 * pas:512 * (pas + 1)].rearrange("n p q -> p n q")),
                waits=[], inc=ld_hB)

        def f2_loads(b):
            pas, d = divmod(b, NKC)
            sp = d % 2
            wwd = [ev_wgB]
            if b < 2:
                wwd.append(S.cur(f1_mm))
            if f2_slot_user[sp] is not None:
                wwd.append(f2_slot_user[sp])
            wd_dst = WS[:, 0:11008] if sp == 0 else B[:, 11264:11264 + 11008]
            ev_wd = wgat_dmas(wgatB, 0, NIC * d, NIC, wd_dst, wwd, ld_wd[sp])
            return ev_wd, wd_dst

        f2_pend = {0: f2_loads(0), 1: f2_loads(1)}
        f2_hloads(0)
        for b in range(2 * NKC):
            pas, d = divmod(b, NKC)
            sp = d % 2
            if b == NKC:
                f2_hloads(1)
            ev_wd, wd_dst = f2_pend.pop(b)
            mmw = [ev_wd, hl_evA[pas]]
            if f2_psum_user[sp] is not None:
                mmw.append(f2_psum_user[sp])
            for i in range(NIC):
                wmi = mmw if i == 0 else ([hl_evB[pas]] if i == 64 else [])
                ev_dmm = S.emit("tensor", lambda E, SM, RG, i=i, sp=sp,
                                wd_dst=wd_dst: E.matmul(
                    P_MED[:, 512 * sp:512 * (sp + 1)],
                    wd_dst[:, 128 * i:128 * (i + 1)],
                    h_ap(i), start=(i == 0), stop=(i == NIC - 1)),
                    waits=wmi, inc=f2_mm)
            f2_slot_user[sp] = ev_dmm
            if b + 2 < 2 * NKC:
                f2_pend[b + 2] = f2_loads(b + 2)
            wfe = [ev_dmm, S.cur(st_fo)]
            if ev_stat2 is not None:
                wfe.append(ev_stat2)
            ev_fe = S.emit("scalar", lambda E, SM, RG, sp=sp: E.activation(
                s_fst[:, 512 * sp:512 * (sp + 1)],
                P_MED[:, 512 * sp:512 * (sp + 1)], AF.Copy),
                waits=wfe, inc=f2_ev)
            f2_psum_user[sp] = ev_fe
            ev_fq = S.emit("scalar", lambda E, SM, RG, sp=sp: E.activation(
                s_sq[:, 0:512], s_fst[:, 512 * sp:512 * (sp + 1)], AF.Square),
                inc=f2_ev)
            S.emit("tensor", lambda E, SM, RG, d=d, pas=pas, sp=sp: E.matmul(
                P_BIG[0:1, 512 * pas:512 * (pas + 1)], s_oc[:],
                s_fst[:, 512 * sp:512 * (sp + 1)],
                start=(d == 0), stop=(d == NKC - 1)),
                waits=[ev_fe], inc=stat_mm)
            ev_stat2 = S.emit("tensor", lambda E, SM, RG, d=d, pas=pas: E.matmul(
                P_BIG[0:1, 1024 + 512 * pas:1024 + 512 * (pas + 1)], s_oc[:],
                s_sq[:, 0:512], start=(d == 0), stop=(d == NKC - 1)),
                waits=[ev_fq], inc=stat_mm)
            ev_fo = S.emit("gpsimd", lambda E, SM, RG, d=d, pas=pas, sp=sp: E.dma_start(
                out=fo_d[pas, d], in_=s_fst[:, 512 * sp:512 * (sp + 1)]),
                waits=[ev_fe], inc=st_fo)
            f2_fst_user[sp] = ev_fo
        ev_bc4 = LN_finalize(ev_stat2)

        f3_user = [None, None]
        for d in range(NKC):
            sp = d % 2
            wfx = [S.cur(st_fo), S.cur(st_x1)] if d < 2 else [f3_user[sp]]
            fxb = s_fx3[sp]
            ev_lf = S.emit("sync", lambda E, SM, RG, d=d, fxb=fxb: E.dma_start(
                out=fxb[:, 0:1024].rearrange("p (a q) -> p a q", a=2),
                in_=fo_d[:, d].rearrange("a p q -> p a q")),
                waits=wfx, inc=ld_fx[sp])
            ev_lx = S.emit("sync", lambda E, SM, RG, d=d, fxb=fxb: E.dma_start(
                out=fxb[:, 1024:2048],
                in_=x1_d[128 * d:128 * (d + 1), :]), waits=[], inc=ld_fx[sp])
            wv_ = [S.cur(ld_fx[sp]), ev_bc4, S.cur(st_x2)]
            S.emit("vector", lambda E, SM, RG, fxb=fxb: E.tensor_mul(
                s_sq[:, 0:1024], fxb[:, 0:1024], s_rb[:, 0:1024]),
                waits=wv_, inc=f3_v)
            S.emit("vector", lambda E, SM, RG: E.tensor_sub(
                s_sq[:, 0:1024], s_sq[:, 0:1024], s_mb[:, 0:1024]), inc=f3_v)
            ev_x2 = S.emit("vector", lambda E, SM, RG, fxb=fxb: E.tensor_add(
                s_sq[:, 0:1024], s_sq[:, 0:1024], fxb[:, 1024:2048]), inc=f3_v)
            ev_sx2 = S.emit("sync", lambda E, SM, RG, d=d: E.dma_start(
                out=x2_d[128 * d:128 * (d + 1), :], in_=s_sq[:, 0:1024]),
                waits=[ev_x2], inc=st_x2)
            f3_user[sp] = ev_x2
        out_user = [None, None]
        for tc in range(NT):
            sp = tc % 2
            w = [S.cur(st_x2)]
            if out_user[sp] is not None:
                w.append(S.cur(st_out))
            ev_tl = S.emit("sync", lambda E, SM, RG, tc=tc, sp=sp: E.dma_start_transpose(
                WS[:, 4096 * sp:4096 * (sp + 1)], x2_d[:, 128 * tc:128 * (tc + 1)]),
                waits=w, inc=ld_out[sp])
            ev_so = S.emit("gpsimd", lambda E, SM, RG, tc=tc, sp=sp: E.dma_start(
                out=out_d[128 * tc:128 * (tc + 1), :],
                in_=WS[:, 4096 * sp:4096 * (sp + 1)]),
                waits=[ev_tl], inc=st_out)
            out_user[sp] = ev_so
        S.emit("gpsimd", lambda E, SM, RG: E.nop(), waits=[S.cur(st_out)])

        st2 = ExitStack()
        with st2:
            sems = {name: st2.enter_context(nc.semaphore(name)) for name in S.plan}
            with nc.Block() as block:

                @block.sync
                def _(E):
                    pid = E.partition_id()
                    rho = E.compute_val(pid % 4)
                    bat = E.compute_val(pid // 4)
                    RG = {}
                    RG["coffA"] = E.compute_val(rho * 512)
                    RG["coffB"] = E.compute_val(rho * 512 + 2048)
                    RG["aorow"] = E.compute_val(bat * 4096)
                    S.run("sync", E, sems, RG)

                @block.gpsimd
                def _(E):
                    S.run("gpsimd", E, sems, {})

                @block.tensor
                def _(E):
                    S.run("tensor", E, sems, {})

                @block.vector
                def _(E):
                    S.run("vector", E, sems, {})

                @block.scalar
                def _(E):
                    S.run("scalar", E, sems, {})

    return nc


_NC_CACHE = None


def kernel(**inputs):
    global _NC_CACHE
    in_maps = _host_prep(inputs)
    if _NC_CACHE is None:
        _NC_CACHE = _build()
    res = run_bass_kernel_spmd(_NC_CACHE, in_maps, core_ids=list(range(8)))
    out = np.zeros((BATCH, T, C), dtype=np.float32)
    for c in range(8):
        b, j = c // 4, c % 4
        o = np.asarray(res.results[c]["out"], dtype=np.float32)
        out[b, 512 * j:512 * (j + 1)] = o[0:512]
        out[b, 512 * (j + 4):512 * (j + 5)] = o[512:1024]
    return out


# version 10
# speedup vs baseline: 1.3166x; 1.1757x over previous
import numpy as np
import ml_dtypes
import concourse.bass as bass
import concourse.mybir as mybir
from concourse.bass_utils import run_bass_kernel_spmd
from concourse.bass import IndirectOffsetOnAxis
from contextlib import ExitStack

F32 = mybir.dt.float32
BF16 = mybir.dt.bfloat16
I32 = mybir.dt.int32
AF = mybir.ActivationFunctionType
ALU = mybir.AluOpType
NPBF = ml_dtypes.bfloat16
ds = bass.ds

C = 4096
T = 4096
BATCH = 2
TCORE = 1024
H = 32
NKC = C // 128
NIC = 11008 // 128
NT = TCORE // 128
SCALE = 128.0 ** -0.5
EPS = 1e-5
GT = NKC * NIC
TPC = GT // 8
PREPG = 4
NGRP = 3 * TPC // PREPG
ENGINES = ("tensor", "vector", "scalar", "sync", "gpsimd")



def _pairperm():
    src = np.arange(C).reshape(H, 64, 2).transpose(0, 2, 1)
    return src.reshape(-1)


def _tile_attn_w(w, perm=None):
    if perm is not None:
        w = w[perm]
    wt = np.ascontiguousarray(w.T)
    t = wt.reshape(NKC, 128, NKC, 128).transpose(2, 1, 0, 3)
    return np.ascontiguousarray(t.reshape(NKC, 128, C))


def _tile_q_gateup(q):
    qt = np.ascontiguousarray(q.T)
    t = qt.reshape(NIC, 128, NKC, 128).transpose(0, 2, 1, 3)
    return np.ascontiguousarray(t.reshape(GT, 128, 128))


def _tile_q_down(q):
    qt = np.ascontiguousarray(q.T)
    t = qt.reshape(NKC, 128, NIC, 128).transpose(0, 2, 1, 3)
    return np.ascontiguousarray(t.reshape(GT, 128, 128))


def _sz_gateup(s):
    st = s.T.reshape(NIC, 128, NKC).transpose(0, 2, 1)
    return np.ascontiguousarray(st.reshape(GT, 128))


def _sz_down(s):
    st = s.T.reshape(NKC, 128, NIC).transpose(0, 2, 1)
    return np.ascontiguousarray(st.reshape(GT, 128))


def _host_prep(inp):
    x = np.asarray(inp["L0_attn_x"], dtype=np.float32)
    perm = _pairperm()
    wq_t = _tile_attn_w(np.asarray(inp["L0_attn_wq"], np.float32), perm)
    wk_t = _tile_attn_w(np.asarray(inp["L0_attn_wk"], np.float32), perm)
    wv_t = _tile_attn_w(np.asarray(inp["L0_attn_wv"], np.float32))
    wo_t = _tile_attn_w(np.asarray(inp["L0_attn_wo"], np.float32))

    g_tiles = _tile_q_gateup(np.asarray(inp["L0_ffn_gate_q"], np.int32))
    u_tiles = _tile_q_gateup(np.asarray(inp["L0_ffn_up_q"], np.int32))
    d_tiles = _tile_q_down(np.asarray(inp["L0_ffn_down_q"], np.int32))
    g_s = _sz_gateup(np.asarray(inp["L0_ffn_gate_s"], np.float32))
    g_z = _sz_gateup(np.asarray(inp["L0_ffn_gate_z"], np.float32))
    u_s = _sz_gateup(np.asarray(inp["L0_ffn_up_s"], np.float32))
    u_z = _sz_gateup(np.asarray(inp["L0_ffn_up_z"], np.float32))
    d_s = _sz_down(np.asarray(inp["L0_ffn_down_s"], np.float32))
    d_z = _sz_down(np.asarray(inp["L0_ffn_down_z"], np.float32))

    masks = np.zeros((4, 128, 512), dtype=NPBF)
    uu = np.arange(128)[:, None]
    tt = np.arange(512)[None, :]
    for w in range(4):
        masks[w] = (uu + 128 * w <= tt).astype(NPBF)

    in_maps = []
    for c in range(8):
        b, j = c // 4, c % 4
        blocks = (j, j + 4)
        xs = np.concatenate([x[b, 512 * m:512 * (m + 1)] for m in blocks], axis=0)
        gidx = np.concatenate([np.arange(512 * m, 512 * (m + 1)) for m in blocks])
        cosr = np.cos(gidx.astype(np.float64)).astype(NPBF)
        sinr = np.sin(gidx.astype(np.float64)).astype(NPBF)
        cosb = np.ascontiguousarray(np.broadcast_to(cosr, (128, TCORE)))
        sinb = np.concatenate([np.broadcast_to(-sinr, (64, TCORE)),
                               np.broadcast_to(sinr, (64, TCORE))], axis=0)
        sinb = np.ascontiguousarray(sinb.astype(NPBF))
        sl = slice(TPC * c, TPC * (c + 1))
        qtiles = np.concatenate([g_tiles[sl], u_tiles[sl], d_tiles[sl]], axis=0)
        qtiles = np.ascontiguousarray(
            qtiles.reshape(3 * TPC // PREPG, PREPG, 128, 128)
            .transpose(0, 2, 1, 3).reshape(3 * TPC // PREPG, 128, PREPG * 128))
        dq_s = np.concatenate([g_s[sl], u_s[sl], d_s[sl]], axis=0).T
        dq_z = np.concatenate([g_z[sl], u_z[sl], d_z[sl]], axis=0).T
        asm_idx = np.zeros((128, 96), np.int32)
        for su in range(8):
            for r in range(4):
                base = 1024 * j + 128 * su + np.arange(128)
                asm_idx[:, su * 12 + r] = 4096 * (4 * b + r) + base
                asm_idx[:, su * 12 + 4 + r] = 8192 * (4 * b + r) + base
                asm_idx[:, su * 12 + 8 + r] = 8192 * (4 * b + r) + 4096 + base
        in_maps.append(dict(
            x_slice=np.ascontiguousarray(xs),
            asm_idx=asm_idx,
            wq_t=wq_t, wk_t=wk_t, wv_t=wv_t, wo_t=wo_t,
            gq=np.ascontiguousarray(qtiles),
            deq_s=np.ascontiguousarray(dq_s), deq_z=np.ascontiguousarray(dq_z),
            rope_cos=cosb, rope_sin=sinb, mask01=masks,
            ones_col=np.ones((128, 1), dtype=NPBF),
            ones_row=np.ones((1, 128), dtype=NPBF),
            eye=np.eye(128, dtype=NPBF),
            jmat=np.ascontiguousarray(np.roll(np.eye(128, dtype=NPBF), 64, axis=1)),
        ))
    return in_maps



class Sched:
    def __init__(self):
        self.steps = {e: [] for e in ENGINES}
        self.plan = {}
        self.isdma = {}
        self.owner = {}

    def sem(self, name, dma=False):
        assert name not in self.plan
        self.plan[name] = 0
        self.isdma[name] = dma
        self.owner[name] = None
        return name

    def cur(self, name):
        return (name, self.plan[name])

    def emit(self, eng, fn, waits=(), inc=None):
        ev = None
        if inc is not None:
            amt = 16 if self.isdma[inc] else 1
            if self.owner[inc] is None:
                self.owner[inc] = eng
            assert self.owner[inc] == eng, f"{inc}: {self.owner[inc]} vs {eng}"
            self.plan[inc] += amt
            ev = (inc, self.plan[inc], amt)
        ww = []
        for w in waits:
            if w is None:
                continue
            sn, v = w[0], w[1]
            if self.isdma[sn]:
                assert v == self.plan[sn], f"partial dma wait {sn}: {v} != {self.plan[sn]}"
            ww.append((sn, v))
        self.steps[eng].append((fn, tuple(ww), ev))
        return (ev[0], ev[1]) if ev else None

    def run(self, eng, E, SM, RG):
        lastw = {}
        for idx, (fn, waits, ev) in enumerate(self.steps[eng]):
            for (sn, v) in waits:
                if lastw.get(sn, -1) < v:
                    E.wait_ge(SM[sn], v)
                    lastw[sn] = v
            try:
                inst = fn(E, SM, RG)
            except Exception:
                import sys
                print(f"Sched.run failed at {eng} step {idx}/{len(self.steps[eng])}",
                      file=sys.stderr)
                raise
            if ev is not None:
                sn, _v, amt = ev
                if amt == 16:
                    inst.then_inc(SM[sn], 16)
                else:
                    inst.then_inc(SM[sn])



def _build():
    nc = bass.Bass("TRN2", target_bir_lowering=False)
    x_in = nc.dram_tensor("x_slice", [TCORE, C], F32, kind="ExternalInput").ap()
    w_t = {n: nc.dram_tensor(n, [NKC, 128, C], F32, kind="ExternalInput").ap()
           for n in ("wq_t", "wk_t", "wv_t", "wo_t")}
    gq = nc.dram_tensor("gq", [NGRP, 128, PREPG * 128], I32, kind="ExternalInput").ap()
    asm_idx_d = nc.dram_tensor("asm_idx", [128, 96], I32, kind="ExternalInput").ap()
    deq_s_d = nc.dram_tensor("deq_s", [128, 3 * TPC], F32, kind="ExternalInput").ap()
    deq_z_d = nc.dram_tensor("deq_z", [128, 3 * TPC], F32, kind="ExternalInput").ap()
    cos_d = nc.dram_tensor("rope_cos", [128, TCORE], BF16, kind="ExternalInput").ap()
    sin_d = nc.dram_tensor("rope_sin", [128, TCORE], BF16, kind="ExternalInput").ap()
    mask_d = nc.dram_tensor("mask01", [4, 128, 512], BF16, kind="ExternalInput").ap()
    oc_d = nc.dram_tensor("ones_col", [128, 1], BF16, kind="ExternalInput").ap()
    or_d = nc.dram_tensor("ones_row", [1, 128], BF16, kind="ExternalInput").ap()
    eye_d = nc.dram_tensor("eye", [128, 128], BF16, kind="ExternalInput").ap()
    jmat_d = nc.dram_tensor("jmat", [128, 128], BF16, kind="ExternalInput").ap()
    out_d = nc.dram_tensor("out", [TCORE, C], F32, kind="ExternalOutput").ap()

    xbf_d = nc.dram_tensor("xbf", [TCORE, C], BF16).ap()
    x0fm_d = nc.dram_tensor("x0fm", [C, TCORE], BF16).ap()
    x1_d = nc.dram_tensor("x1d", [C, TCORE], BF16).ap()
    x2_d = nc.dram_tensor("x2d", [C, TCORE], BF16).ap()
    kvq_con = nc.dram_tensor("kvq_con", [3 * C, TCORE], BF16).ap()
    kv_gat = nc.dram_tensor("kv_gat", [8 * 2 * C, TCORE], BF16, addr_space="Shared").ap()
    q_gat = nc.dram_tensor("q_gat", [8 * C, TCORE], BF16, addr_space="Shared").ap()
    wconA = nc.dram_tensor("wconA", [128, 2 * TPC * 128], BF16).ap()
    wgatA = nc.dram_tensor("wgatA", [8 * 128, 2 * TPC * 128], BF16, addr_space="Shared").ap()
    wconB = nc.dram_tensor("wconB", [128, TPC * 128], BF16).ap()
    wgatB = nc.dram_tensor("wgatB", [8 * 128, TPC * 128], BF16, addr_space="Shared").ap()
    ao_con = nc.dram_tensor("ao_con", [8 * 128, T], BF16).ap()
    ao_gat = nc.dram_tensor("ao_gat", [8 * 8 * 128, T], BF16, addr_space="Shared").ap()
    h_d = nc.dram_tensor("h_d", [NIC, 128, 1024], BF16).ap()
    fo_d = nc.dram_tensor("fo_d", [2, NKC, 128, 512], BF16).ap()

    S = Sched()
    sm = lambda n: S.sem(n, dma=False)
    dm = lambda n: S.sem(n, dma=True)

    st = ExitStack()
    with st:
        E_ = st.enter_context
        A = E_(nc.sbuf_tensor("A", [128, 32768], BF16))
        B = E_(nc.sbuf_tensor("Bb", [128, 32768], BF16))
        WS = E_(nc.sbuf_tensor("WS", [128, 11264], BF16))
        PQ = E_(nc.sbuf_tensor("PQ", [128, 1024], I32))
        PD = E_(nc.sbuf_tensor("PD", [128, 1024], BF16))
        PN = E_(nc.sbuf_tensor("PN", [128, 1024], BF16))
        s_ds = E_(nc.sbuf_tensor("s_ds", [128, 3 * TPC], F32))
        s_dz = E_(nc.sbuf_tensor("s_dz", [128, 3 * TPC], F32))
        s_cos = E_(nc.sbuf_tensor("s_cos", [128, TCORE], BF16))
        s_sin = E_(nc.sbuf_tensor("s_sin", [128, TCORE], BF16))
        s_oc = E_(nc.sbuf_tensor("s_oc", [128, 1], BF16))
        s_or = E_(nc.sbuf_tensor("s_or", [1, 128], BF16))
        s_eye = E_(nc.sbuf_tensor("s_eye", [128, 128], BF16))
        s_j = E_(nc.sbuf_tensor("s_j", [128, 128], BF16))
        s_mb = E_(nc.sbuf_tensor("s_mb", [128, 1024], BF16))
        s_rb = E_(nc.sbuf_tensor("s_rb", [128, 1024], BF16))
        s_rows = E_(nc.sbuf_tensor("s_rows", [1, 2048], F32))
        s_msq = E_(nc.sbuf_tensor("s_msq", [1, 1024], F32))
        s_rbf = E_(nc.sbuf_tensor("s_rbf", [1, 1024], BF16))
        s_bnd = E_(nc.sbuf_tensor("s_bnd", [1, 64], F32))
        s_eps = E_(nc.sbuf_tensor("s_eps", [1, 2], F32))
        s_bia = E_(nc.sbuf_tensor("s_bia", [128, 2], F32))
        s_bnd2 = E_(nc.sbuf_tensor("s_bnd2", [128, 12], F32))
        s_bndb = E_(nc.sbuf_tensor("s_bndb", [128, 1], BF16))
        s_aidx = E_(nc.sbuf_tensor("s_aidx", [128, 96], I32))
        s_qe = E_(nc.sbuf_tensor("s_qe", [128, 2048], BF16))
        s_qr = E_(nc.sbuf_tensor("s_qr", [128, 2048], BF16))
        s_sq = E_(nc.sbuf_tensor("s_sq", [128, 1024], BF16))
        s_tr1 = s_sq
        s_sil = E_(nc.sbuf_tensor("s_sil", [128, 1024], BF16))
        s_hst = E_(nc.sbuf_tensor("s_hst", [128, 1024], BF16))
        s_fst = E_(nc.sbuf_tensor("s_fst", [128, 1024], BF16))
        s_recb = E_(nc.sbuf_tensor("s_recb", [128, 512], BF16))

        s_pb = B[:, 0:4096]
        s_ao = B[:, 4096:5120]
        s_msk = B[:, 5120:7168]
        s_x0s = s_hst
        s_fx3 = (s_qe, s_qr)

        P_BIG = E_(nc.psum_tensor("P_BIG", [128, 2048], F32))
        P_MED = E_(nc.psum_tensor("P_MED", [128, 1024], F32))
        P_BF = E_(nc.psum_tensor("P_BF", [128, 1024], BF16))
        P_AUX = E_(nc.psum_tensor("P_AUX", [128, 512], F32))

        ld_const = dm("ld_const")
        zp_done = sm("zp_done")
        ld_prep = [dm(f"ld_prep{i}") for i in range(2)]
        prep_deq = sm("prep_deq")
        prep_tp = sm("prep_tp")
        prep_ev = sm("prep_ev")
        prep_st = dm("prep_st")
        cc_done = sm("cc_done")
        ld_x = [dm(f"ld_x{i}") for i in range(2)]
        st_xbf = dm("st_xbf")
        ld_xfm = dm("ld_xfm")
        sq_done = sm("sq_done")
        stat_mm = sm("stat_mm")
        ln_fin = sm("ln_fin")
        bc_ev = sm("bc_ev")
        norm_done = sm("norm_done")
        st_x0 = dm("st_x0")
        ld_w = [dm(f"ld_w{i}") for i in range(2)]
        qkv_mm = sm("qkv_mm")
        qkv_ev = sm("qkv_ev")
        rope_done = sm("rope_done")
        st_con = [dm("st_con0"), dm("st_con1")]
        ld_slot = [dm(f"ld_slot{i}") for i in range(2)]
        ld_slq = [dm(f"ld_slq{i}") for i in range(2)]
        vtp = sm("vtp")
        vtp_ev = sm("vtp_ev")
        bnd_sq = sm("bnd_sq")
        bnd_mm = sm("bnd_mm")
        bnd_ev = sm("bnd_ev")
        bnd_red = sm("bnd_red")
        att_s = sm("att_s")
        att_e = sm("att_e")
        att_m = sm("att_m")
        att_pv = sm("att_pv")
        att_ev = sm("att_ev")
        rs_mm = sm("rs_mm")
        rs_cp = sm("rs_cp")
        rs_ve = sm("rs_ve")
        st_ao = dm("st_ao")
        ld_attn = dm("ld_attn")
        o_mm = sm("o_mm")
        ld_x0s = [dm("ld_x0s0")]
        res1 = sm("res1")
        st_x1 = dm("st_x1")
        ld_fg = [dm(f"ld_fg{i}") for i in range(2)]
        ld_fu = [dm(f"ld_fu{i}") for i in range(2)]
        f1_mm = sm("f1_mm")
        f1_sil = sm("f1_sil")
        f1_mul = sm("f1_mul")
        st_h = dm("st_h")
        ld_hA = dm("ld_hA")
        ld_hB = dm("ld_hB")
        ld_wd = [dm(f"ld_wd{i}") for i in range(2)]
        f2_mm = sm("f2_mm")
        f2_ev = sm("f2_ev")
        st_fo = dm("st_fo")
        ld_fx = [dm(f"ld_fx{i}") for i in range(2)]
        f3_v = sm("f3_v")
        st_x2 = dm("st_x2")
        ld_out = [dm(f"ld_out{i}") for i in range(2)]
        st_out = dm("st_out")

        for (dst, src) in ((s_ds, deq_s_d), (s_dz, deq_z_d), (s_cos, cos_d),
                           (s_sin, sin_d), (s_oc, oc_d), (s_or, or_d),
                           (s_eye, eye_d), (s_j, jmat_d)):
            S.emit("gpsimd", lambda E, SM, RG, d=dst, s=src: E.dma_start(out=d[:], in_=s),
                   inc=ld_const)
        S.emit("gpsimd", lambda E, SM, RG: E.dma_start(
            out=s_aidx[:], in_=asm_idx_d), inc=ld_const)
        ev_const = S.cur(ld_const)
        ev_zp = S.emit("vector", lambda E, SM, RG: E.scalar_tensor_tensor(
            s_dz[:], s_ds[:], -8.0, s_dz[:], ALU.mult, ALU.add),
            waits=[ev_const], inc=zp_done)
        eps_done = sm("eps_done")
        S.emit("vector", lambda E, SM, RG: E.memset(s_eps[0:1, 0:1], EPS), inc=eps_done)
        S.emit("vector", lambda E, SM, RG: E.memset(s_bnd2[:], 0.0), inc=eps_done)
        ev_eps = S.emit("vector", lambda E, SM, RG: E.memset(s_eps[0:1, 1:2], 1e-12),
                        inc=eps_done)

        ev_xst = None
        for tc in range(NT):
            sp = tc % 2
            w = [S.cur(st_xbf)] if tc >= 2 else []
            ev_l = S.emit("gpsimd", lambda E, SM, RG, tc=tc, sp=sp: E.dma_start(
                out=A[:, 4096 * sp:4096 * (sp + 1)], in_=x_in[128 * tc:128 * (tc + 1), :]),
                waits=w, inc=ld_x[sp])
            ev_xst = S.emit("gpsimd", lambda E, SM, RG, tc=tc, sp=sp: E.dma_start(
                out=xbf_d[128 * tc:128 * (tc + 1), :], in_=A[:, 4096 * sp:4096 * (sp + 1)]),
                waits=[ev_l], inc=st_xbf)
        for k in range(NKC):
            S.emit("scalar", lambda E, SM, RG, k=k: E.dma_start_transpose(
                B[:, 1024 * k:1024 * (k + 1)], xbf_d[:, 128 * k:128 * (k + 1)]),
                waits=[ev_xst], inc=ld_xfm)
        ev_xfm_all = S.cur(ld_xfm)

        def LN_emit(src, wait_chunk=None, first_waits=()):
            prev_s2 = None
            for k in range(NKC):
                wsq = list(first_waits) if k == 0 else []
                if wait_chunk is not None:
                    wsq.append(wait_chunk(k))
                if prev_s2 is not None:
                    wsq.append(prev_s2)
                ev_sq = S.emit("scalar", lambda E, SM, RG, k=k: E.activation(
                    s_sq[:, 0:1024], src(k), AF.Square), waits=wsq, inc=sq_done)
                for hf in range(2):
                    S.emit("tensor", lambda E, SM, RG, k=k, hf=hf: E.matmul(
                        P_BIG[0:1, 512 * hf:512 * (hf + 1)], s_oc[:],
                        src(k)[:, 512 * hf:512 * (hf + 1)],
                        start=(k == 0), stop=(k == NKC - 1)),
                        waits=(wsq + [ev_const]) if hf == 0 else [], inc=stat_mm)
                for hf in range(2):
                    prev_s2 = S.emit("tensor", lambda E, SM, RG, k=k, hf=hf: E.matmul(
                        P_BIG[0:1, 1024 + 512 * hf:1024 + 512 * (hf + 1)], s_oc[:],
                        s_sq[:, 512 * hf:512 * (hf + 1)],
                        start=(k == 0), stop=(k == NKC - 1)),
                        waits=[ev_sq] if hf == 0 else [], inc=stat_mm)
            return LN_finalize(prev_s2)

        def LN_finalize(ev_mm):
            ev_f1 = S.emit("scalar", lambda E, SM, RG: E.activation(
                s_rows[:, 0:1024], P_BIG[0:1, 0:1024], AF.Copy, scale=1.0 / C),
                waits=[ev_mm], inc=bc_ev)
            ev_f2 = S.emit("scalar", lambda E, SM, RG: E.activation(
                s_rows[:, 1024:2048], P_BIG[0:1, 1024:2048], AF.Copy, scale=1.0 / C),
                inc=bc_ev)
            ev_a = S.emit("vector", lambda E, SM, RG: E.tensor_mul(
                s_msq[:], s_rows[0:1, 0:1024], s_rows[0:1, 0:1024]),
                waits=[ev_f1], inc=ln_fin)
            ev_b = S.emit("vector", lambda E, SM, RG: E.tensor_sub(
                s_msq[:], s_rows[0:1, 1024:2048], s_msq[:]),
                waits=[ev_f2, ev_a], inc=ln_fin)
            ev_c = S.emit("scalar", lambda E, SM, RG: E.activation(
                s_msq[:], s_msq[:], AF.Ln, bias=s_eps[0:1, 0:1]),
                waits=[ev_b, ev_eps], inc=bc_ev)
            ev_d = S.emit("scalar", lambda E, SM, RG: E.activation(
                s_msq[:], s_msq[:], AF.Exp, scale=-0.5), inc=bc_ev)
            ev_e = S.emit("vector", lambda E, SM, RG: E.tensor_mul(
                s_rows[0:1, 0:1024], s_rows[0:1, 0:1024], s_msq[:]),
                waits=[ev_d], inc=ln_fin)
            ev_g = S.emit("vector", lambda E, SM, RG: E.tensor_copy(
                s_rbf[0:1, 0:1024], s_msq[0:1, 0:1024]), waits=[ev_d], inc=ln_fin)
            ev_prev = None
            for hf in range(2):
                wv_ = [ev_g]
                if ev_prev is not None:
                    wv_.append(ev_prev)
                ev_bm = S.emit("tensor", lambda E, SM, RG, hf=hf: E.matmul(
                    P_AUX[:], s_or[:], s_rbf[0:1, 512 * hf:512 * (hf + 1)],
                    start=True, stop=True), waits=wv_, inc=stat_mm)
                ev_prev = S.emit("scalar", lambda E, SM, RG, hf=hf: E.activation(
                    s_rb[:, 512 * hf:512 * (hf + 1)], P_AUX[:], AF.Copy),
                    waits=[ev_bm], inc=bc_ev)
            ev_h = S.emit("vector", lambda E, SM, RG: E.tensor_copy(
                s_rbf[0:1, 0:1024], s_rows[0:1, 0:1024]),
                waits=[ev_e, ev_prev], inc=ln_fin)
            for hf in range(2):
                ev_bm = S.emit("tensor", lambda E, SM, RG, hf=hf: E.matmul(
                    P_AUX[:], s_or[:], s_rbf[0:1, 512 * hf:512 * (hf + 1)],
                    start=True, stop=True), waits=[ev_h, ev_prev], inc=stat_mm)
                ev_prev = S.emit("scalar", lambda E, SM, RG, hf=hf: E.activation(
                    s_mb[:, 512 * hf:512 * (hf + 1)], P_AUX[:], AF.Copy),
                    waits=[ev_bm], inc=bc_ev)
            return ev_prev

        def LN_norm(src_ap, dst_ap, ev_bc, extra=()):
            S.emit("vector", lambda E, SM, RG: E.tensor_mul(
                dst_ap, src_ap, s_rb[:, 0:1024]),
                waits=[ev_bc] + list(extra), inc=norm_done)
            return S.emit("vector", lambda E, SM, RG: E.tensor_sub(
                dst_ap, dst_ap, s_mb[:, 0:1024]), inc=norm_done)

        ev_bc1 = LN_emit(lambda k: B[:, 1024 * k:1024 * (k + 1)],
                         first_waits=[ev_xfm_all])
        for k in range(NKC):
            ev_ln1 = LN_norm(B[:, 1024 * k:1024 * (k + 1)],
                             A[:, 1024 * k:1024 * (k + 1)], ev_bc1)
        ev_x0sp = S.emit("scalar", lambda E, SM, RG: E.dma_start(
            out=x0fm_d[:].rearrange("(k p) t -> p k t", p=128),
            in_=B[:].rearrange("p (k t) -> p k t", k=NKC)),
            waits=[ev_ln1], inc=st_x0)

        def w_slot(sp):
            return WS[:, 4096 * sp: 4096 * (sp + 1)]

        wstate = {"count": 0, "slot_user": [None, None]}

        def load_wblock(name, n):
            sp = wstate["count"] % 2
            w = [wstate["slot_user"][sp]] if wstate["slot_user"][sp] else []
            ev = S.emit("gpsimd", lambda E, SM, RG, name=name, n=n, sp=sp: E.dma_start(
                out=w_slot(sp), in_=w_t[name][n]),
                waits=w, inc=ld_w[sp])
            wstate["count"] += 1
            return ev, sp

        pstate = {"slot_user": [None, None]}

        def proj_psum_waits(qp):
            return [pstate["slot_user"][qp]] if pstate["slot_user"][qp] else []

        gidx = [0]
        qkvs = {"qe_user": [None, None], "qr_user": [None, None],
                "jpsum_user": [None, None], "pending": None}

        def qkv_one(name, n, crow, rope):
            ev_w, sp = load_wblock(name, n)
            qp = gidx[0] % 2
            gidx[0] += 1
            mmw = [ev_w, ev_ln1] + proj_psum_waits(qp)
            for k in range(NKC):
                if k == 8 and qkvs["pending"] is not None:
                    qkvs["pending"]()
                    qkvs["pending"] = None
                for hf in range(2):
                    ev_mm = S.emit("tensor", lambda E, SM, RG, k=k, hf=hf, sp=sp, qp=qp: E.matmul(
                        P_BIG[:, 1024 * qp + 512 * hf:1024 * qp + 512 * (hf + 1)],
                        w_slot(sp)[:, 128 * k:128 * (k + 1)],
                        A[:, 1024 * k + 512 * hf:1024 * k + 512 * (hf + 1)],
                        start=(k == 0), stop=(k == NKC - 1)),
                        waits=mmw if (k == 0 and hf == 0) else [], inc=qkv_mm)
            wstate["slot_user"][sp] = ev_mm
            evw = [ev_mm, S.cur(st_con[qp])]
            if qkvs["qe_user"][qp] is not None:
                evw.append(qkvs["qe_user"][qp])
            ev_ev = S.emit("scalar", lambda E, SM, RG, qp=qp: E.activation(
                s_qe[:, 1024 * qp:1024 * (qp + 1)], P_BIG[:, 1024 * qp:1024 * (qp + 1)],
                AF.Copy), waits=evw, inc=qkv_ev)
            pstate["slot_user"][qp] = ev_ev
            if rope:
                def emit_jmm(qp=qp, ev_ev=ev_ev):
                    wj = [ev_ev, ev_const]
                    ju = qkvs["jpsum_user"][0]
                    if ju is not None:
                        wj.append(ju)
                    for hf in range(2):
                        ev_jmm = S.emit("tensor", lambda E, SM, RG, qp=qp, hf=hf: E.matmul(
                            P_MED[:, 512 * hf:512 * (hf + 1)], s_j[:],
                            s_qe[:, 1024 * qp + 512 * hf:1024 * qp + 512 * (hf + 1)],
                            start=True, stop=True),
                            waits=wj if hf == 0 else [], inc=qkv_mm)
                    wr = [ev_ev, S.cur(st_con[qp])]
                    S.emit("vector", lambda E, SM, RG, qp=qp: E.tensor_mul(
                        s_tr1[:, :], s_qe[:, 1024 * qp:1024 * (qp + 1)], s_cos[:]),
                        waits=wr, inc=rope_done)
                    ev_j2 = S.emit("vector", lambda E, SM, RG, qp=qp: E.tensor_mul(
                        s_qr[:, 1024 * qp:1024 * (qp + 1)], P_MED[:, 0:1024], s_sin[:]),
                        waits=[ev_jmm], inc=rope_done)
                    qkvs["jpsum_user"][0] = ev_j2
                    ev_src = S.emit("vector", lambda E, SM, RG, qp=qp: E.tensor_add(
                        s_qr[:, 1024 * qp:1024 * (qp + 1)],
                        s_qr[:, 1024 * qp:1024 * (qp + 1)], s_tr1[:, :]),
                        inc=rope_done)
                    qkvs["qe_user"][qp] = ev_src
                    ev_st = S.emit("sync", lambda E, SM, RG, crow=crow, n=n, qp=qp: E.dma_start(
                        out=kvq_con[crow + 128 * n: crow + 128 * (n + 1), :],
                        in_=s_qr[:, 1024 * qp:1024 * (qp + 1)]),
                        waits=[ev_src], inc=st_con[qp])
                qkvs["pending"] = emit_jmm
            else:
                ev_st = S.emit("sync", lambda E, SM, RG, crow=crow, n=n, qp=qp: E.dma_start(
                    out=kvq_con[crow + 128 * n: crow + 128 * (n + 1), :],
                    in_=s_qe[:, 1024 * qp:1024 * (qp + 1)]),
                    waits=[ev_ev], inc=st_con[qp])
                qkvs["qe_user"][qp] = None

        for n in range(NKC):
            qkv_one("wk_t", n, C, True)
        for n in range(NKC):
            qkv_one("wv_t", n, 2 * C, False)
        if qkvs["pending"] is not None:
            qkvs["pending"]()
            qkvs["pending"] = None
        ev_agKV = S.emit("gpsimd", lambda E, SM, RG: E.collective_compute(
            "AllGather", ALU.bypass, replica_groups=[[0, 1, 2, 3, 4, 5, 6, 7]],
            ins=[kvq_con[C:3 * C, :].opt()], outs=[kv_gat.opt()]),
            waits=[S.cur(st_con[0]), S.cur(st_con[1])], inc=cc_done)
        for n in range(NKC):
            qkv_one("wq_t", n, 0, True)
        if qkvs["pending"] is not None:
            qkvs["pending"]()
            qkvs["pending"] = None
        ev_agQ = S.emit("gpsimd", lambda E, SM, RG: E.collective_compute(
            "AllGather", ALU.bypass, replica_groups=[[0, 1, 2, 3, 4, 5, 6, 7]],
            ins=[kvq_con[0:C, :].opt()], outs=[q_gat.opt()]),
            waits=[S.cur(st_con[0]), S.cur(st_con[1])], inc=cc_done)
        ev_agA = ev_agQ
        pq_user = [None, None]
        pd_user = [None, None]
        pn_user = [None, None]
        for grp in range(NGRP):
            sp = grp % 2
            t0 = grp * PREPG
            w = [pq_user[sp]] if pq_user[sp] else []
            ev_ld = S.emit("gpsimd", lambda E, SM, RG, grp=grp, sp=sp: E.dma_start(
                out=PQ[:, 512 * sp:512 * (sp + 1)], in_=gq[grp]),
                waits=w, inc=ld_prep[sp])
            wdeq = [ev_ld, ev_zp]
            if pd_user[sp]:
                wdeq.append(pd_user[sp])
            for i in range(PREPG):
                t = t0 + i
                ev_dq = S.emit("vector", lambda E, SM, RG, i=i, sp=sp, t=t: E.tensor_scalar(
                    PD[:, 512 * sp + 128 * i:512 * sp + 128 * (i + 1)],
                    PQ[:, 512 * sp + 128 * i:512 * sp + 128 * (i + 1)],
                    s_ds[:, t:t + 1], s_dz[:, t:t + 1], ALU.mult, ALU.add),
                    waits=wdeq if i == 0 else [], inc=prep_deq)
            pq_user[sp] = ev_dq
            wtp = [ev_dq, ev_const]
            if grp >= 1:
                wtp.append(S.cur(prep_ev))
            for i in range(PREPG):
                ev_tp = S.emit("tensor", lambda E, SM, RG, i=i, sp=sp: E.transpose(
                    P_BF[:, 128 * i:128 * (i + 1)],
                    PD[:, 512 * sp + 128 * i:512 * sp + 128 * (i + 1)], s_eye[:]),
                    waits=wtp if i == 0 else [], inc=prep_tp)
            pd_user[sp] = ev_tp
            wev = [ev_tp]
            if pn_user[sp]:
                wev.append(S.cur(prep_st))
            ev_ev = S.emit("vector", lambda E, SM, RG, sp=sp: E.tensor_copy(
                PN[:, 512 * sp:512 * (sp + 1)], P_BF[:, 0:512]), waits=wev, inc=prep_ev)
            dst = (wconA[:, 128 * t0:128 * t0 + 512] if t0 < 2 * TPC
                   else wconB[:, 128 * (t0 - 2 * TPC):128 * (t0 - 2 * TPC) + 512])
            ev_st = S.emit("sync", lambda E, SM, RG, dst=dst, sp=sp: E.dma_start(
                out=dst, in_=PN[:, 512 * sp:512 * (sp + 1)]),
                waits=[ev_ev], inc=prep_st)
            pn_user[sp] = ev_st


        ev_wgA = S.emit("gpsimd", lambda E, SM, RG: E.collective_compute(
            "AllGather", ALU.bypass, replica_groups=[[0, 1, 2, 3, 4, 5, 6, 7]],
            ins=[wconA.opt()], outs=[wgatA.opt()]),
            waits=[ev_prepA if False else S.cur(prep_st)], inc=cc_done)


        def tcol(sp, ti, m):
            return 12288 * sp + 3072 * (m % 4) + 1024 * ti + 512 * (m // 4)

        def q_blk(sp, m):
            c0 = tcol(sp, 0, m)
            return A[:, c0:c0 + 512]

        def k_chk(sp, c):
            c0 = tcol(sp, 1, c // 4) + 128 * (c % 4)
            return A[:, c0:c0 + 128]

        def v_chk(sp, c):
            c0 = tcol(sp, 2, c // 4) + 128 * (c % 4)
            return A[:, c0:c0 + 128]

        unit_end = [None, None]
        prep_ev_total = S.cur(prep_ev)
        rope_total = S.cur(rope_done)
        bcev_total = S.cur(bc_ev)
        qkvev_total = S.cur(qkv_ev)
        ld_msk = dm("ld_msk")
        ev_mload = S.emit("gpsimd", lambda E, SM, RG: E.dma_start(
            out=s_msk.rearrange("p (w q) -> p w q", w=4),
            in_=mask_d.rearrange("w p q -> p w q")),
            waits=[S.cur(st_x0)], inc=ld_msk)

        asm_kv = {}
        asm_q = {}

        def emit_assembly(su):
            sp = su % 2
            w0 = [unit_end[sp]] if unit_end[sp] is not None else []
            order = [(ti, r_) for ti in (1, 2) for r_ in range(4)] + \
                    [(0, r_) for r_ in range(4)]
            for nn, (ti, r_) in enumerate(order):
                gat = q_gat if ti == 0 else kv_gat
                dstc = 12288 * sp + 3072 * r_ + 1024 * ti
                icol = 12 * su + (r_ if ti == 0 else 4 * ti + r_)
                wj = list(w0) if nn == 0 else []
                if nn == 0:
                    wj.append(ev_agKV)
                if (ti, r_) == (0, 0):
                    wj = list(w0) + [ev_agQ]
                S.emit("gpsimd", lambda E, SM, RG, ti=ti, r_=r_, dstc=dstc,
                       gat=gat, icol=icol: E.indirect_dma_start(
                    A[:, dstc:dstc + 1024], None, gat,
                    IndirectOffsetOnAxis(s_aidx[:, icol:icol + 1], 0)),
                    waits=wj, inc=(ld_slq[sp] if ti == 0 else ld_slot[sp]))
            asm_kv[su] = S.cur(ld_slot[sp])
            asm_q[su] = S.cur(ld_slq[sp])

        def emit_vt_group(su, g):
            sp = su % 2
            wv_ = [asm_kv[su], S.cur(vtp_ev), S.cur(bnd_red)]
            if su == 0 and g == 0:
                wv_.append(prep_ev_total)
            for i in range(8):
                c = 8 * g + i
                ev_t = S.emit("tensor", lambda E, SM, RG, c=c, i=i, sp=sp: E.transpose(
                    P_BF[:, 128 * i:128 * (i + 1)], v_chk(sp, c), s_eye[:]),
                    waits=wv_ if i == 0 else [], inc=vtp)
            for i in range(8):
                c = 8 * g + i
                S.emit("vector", lambda E, SM, RG, c=c, i=i, sp=sp: E.tensor_copy(
                    v_chk(sp, c), P_BF[:, 128 * i:128 * (i + 1)]),
                    waits=[ev_t] if i == 0 else [], inc=vtp_ev)

        def emit_bounds_side(su, ti, fb, winw):
            sp = su % 2
            ev_red = {}
            ev_mm2 = {}
            for qb in range(4):
                cb = 12288 * sp + 3072 * qb + 1024 * ti
                par = qb % 2
                wsq = [asm_q[su] if ti == 0 else asm_kv[su], S.cur(bnd_mm)] \
                    if qb < 2 else [ev_mm2[qb - 2]]
                ev_bsq = S.emit("vector", lambda E, SM, RG, cb=cb, par=par: E.tensor_mul(
                    s_qe[:, 1024 * par:1024 * (par + 1)],
                    A[:, cb:cb + 1024], A[:, cb:cb + 1024]),
                    waits=wsq, inc=bnd_sq)
                for g in range(2):
                    base = 64 * g
                    wmm = [ev_bsq]
                    if qb == 0 and g == 0:
                        wmm += winw
                    if qb >= 1:
                        wmm.append(ev_red[(qb - 1, g)])
                    ev_bmm = S.emit("tensor", lambda E, SM, RG, base=base, g=g, fb=fb,
                                    par=par: E.matmul(
                        P_MED[base:base + 1, 512 * fb:512 * (fb + 1)], s_oc[:],
                        s_qe[:, 1024 * par + 512 * g:1024 * par + 512 * (g + 1)],
                        start=True, stop=True), waits=wmm, inc=bnd_mm)
                    ev_red[(qb, g)] = S.emit("vector", lambda E, SM, RG, base=base,
                                             fb=fb, ti=ti, qb=qb: E.reduce_max(
                        s_bnd2[base:base + 1, 4 * ti + qb:4 * ti + qb + 1],
                        P_MED[base:base + 1, 512 * fb:512 * (fb + 1)],
                        axis=mybir.AxisListType.X), waits=[ev_bmm], inc=bnd_red)
                ev_mm2[qb] = ev_bmm
            for base in (0, 64):
                ev_cm = S.emit("vector", lambda E, SM, RG, ti=ti, base=base: E.reduce_max(
                    s_bnd2[base:base + 1, 8 + ti:9 + ti],
                    s_bnd2[base:base + 1, 4 * ti:4 * ti + 4],
                    axis=mybir.AxisListType.X), inc=bnd_red)
            ev_cb = S.emit("vector", lambda E, SM, RG, ti=ti: E.tensor_copy(
                s_bndb[:, 0:1], s_bnd2[:, 8 + ti:9 + ti]),
                waits=[ev_cm], inc=bnd_red)
            ev_tr = S.emit("tensor", lambda E, SM, RG: E.transpose(
                P_BF[0:1, 0:128], s_bndb[:, 0:1], s_eye[:]),
                waits=[ev_cb, S.cur(vtp_ev), S.cur(bnd_red)], inc=bnd_mm)
            return S.emit("vector", lambda E, SM, RG, ti=ti: E.reduce_max(
                s_bnd[0:1, 16 + 8 * ti:17 + 8 * ti], P_BF[0:1, 0:128],
                axis=mybir.AxisListType.X), waits=[ev_tr], inc=bnd_red)

        def emit_bounds_tail(su, ev_q, ev_k, fbk):
            bcol = su % 2
            ev_b2 = S.emit("vector", lambda E, SM, RG: E.tensor_mul(
                s_bnd[0:1, 32:33], s_bnd[0:1, 16:17], s_bnd[0:1, 24:25]),
                waits=[ev_q, ev_k], inc=bnd_red)
            ev_bl = S.emit("scalar", lambda E, SM, RG: E.activation(
                s_bnd[0:1, 33:34], s_bnd[0:1, 32:33], AF.Ln, bias=s_eps[0:1, 1:2]),
                waits=[ev_b2, ev_eps], inc=bnd_ev)
            S.emit("scalar", lambda E, SM, RG: E.activation(
                s_bnd[0:1, 34:35], s_bnd[0:1, 33:34], AF.Exp, scale=0.5), inc=bnd_ev)
            ev_bs = S.emit("vector", lambda E, SM, RG: E.tensor_scalar_mul(
                s_bnd[0:1, 35:36], s_bnd[0:1, 34:35], -1.02 * SCALE),
                waits=[S.cur(bnd_ev)], inc=bnd_red)
            ev_bb = S.emit("vector", lambda E, SM, RG: E.tensor_copy(
                s_rbf[0:1, 512:513], s_bnd[0:1, 35:36]), waits=[ev_bs], inc=bnd_red)
            ev_bcm = S.emit("tensor", lambda E, SM, RG, fbk=fbk: E.matmul(
                P_MED[:, 512 * fbk:512 * fbk + 1], s_or[:], s_rbf[0:1, 512:513],
                start=True, stop=True),
                waits=[ev_bb, S.cur(bnd_red)], inc=bnd_mm)
            S.emit("scalar", lambda E, SM, RG, fbk=fbk, bcol=bcol: E.activation(
                s_bia[:, bcol:bcol + 1], P_MED[:, 512 * fbk:512 * fbk + 1], AF.Copy),
                waits=[ev_bcm], inc=bnd_ev)

        gpx = [None] * 4
        gpb = [None] * 8
        fin_prev = None
        ev_rev_prev = None

        def emit_finalize(fin):
            ev_bcmm = S.emit("tensor", lambda E, SM, RG: E.matmul(
                P_BIG[:, 1536:2048], s_or[:], s_rbf[0:1, 0:512],
                start=True, stop=True), waits=[fin["rb"], fin["exp3"]], inc=rs_mm)
            ev_rev = S.emit("scalar", lambda E, SM, RG: E.activation(
                s_recb[:], P_BIG[:, 1536:2048], AF.Copy), waits=[ev_bcmm], inc=rs_cp)
            ev_ae = S.emit("vector", lambda E, SM, RG, fin=fin: E.tensor_mul(
                s_ao[:, 512 * fin["aop"]:512 * (fin["aop"] + 1)],
                P_MED[:, 512 * fin["mpar"]:512 * (fin["mpar"] + 1)], s_recb[:]),
                waits=[fin["pv"], ev_rev, S.cur(st_ao)], inc=att_ev)
            S.emit("sync", lambda E, SM, RG, fin=fin: E.dma_start(
                out=ao_con[128 * fin["su"]:128 * (fin["su"] + 1),
                           512 * fin["m"]:512 * (fin["m"] + 1)],
                in_=s_ao[:, 512 * fin["aop"]:512 * (fin["aop"] + 1)]),
                waits=[ev_ae], inc=st_ao)
            return ev_rev

        emit_assembly(0)
        emit_assembly(1)
        for g in range(4):
            emit_vt_group(0, g)
        ev_q0 = emit_bounds_side(0, 0, 1, [rope_total])
        ev_k0 = emit_bounds_side(0, 1, 0, [])
        emit_bounds_tail(0, ev_q0, ev_k0, 0)

        for su in range(8):
            sp = su % 2
            if su + 1 < 8:
                emit_assembly(su + 1)
            for m in range(8):
                if su + 1 < 8 and 1 <= m <= 4:
                    emit_vt_group(su + 1, m - 1)
                nch = 4 * (m + 1)
                mpar = m % 2

                def front(c, su=su, sp=sp, m=m):
                    p4 = c % 4
                    p8 = c % 8
                    wS = []
                    if gpx[p4] is not None:
                        wS.append(gpx[p4])
                    if c == 3 and ev_rev_prev is not None:
                        wS.append(ev_rev_prev)
                    if m == 0 and c == 0:
                        wS += [asm_kv[su], asm_q[su]]
                    if su == 0 and m == 0 and c == 0:
                        wS.append(qkvev_total)
                    ev_smm = S.emit("tensor", lambda E, SM, RG, c=c, p4=p4,
                                    sp=sp, m=m: E.matmul(
                        P_BIG[:, 512 * p4:512 * (p4 + 1)],
                        k_chk(sp, c), q_blk(sp, m),
                        start=True, stop=True), waits=wS, inc=att_s)
                    wE = [ev_smm]
                    if gpb[p8] is not None:
                        wE.append(gpb[p8])
                    ev_exp = S.emit("scalar", lambda E, SM, RG, p8=p8, p4=p4: E.activation(
                        s_pb[:, 512 * p8:512 * (p8 + 1)],
                        P_BIG[:, 512 * p4:512 * (p4 + 1)],
                        AF.Exp, bias=s_bia[:, su % 2:su % 2 + 1], scale=SCALE),
                        waits=wE, inc=att_e)
                    gpx[p4] = ev_exp
                    ev_mask = None
                    if c >= 4 * m:
                        wvar = c - 4 * m
                        ev_mask = S.emit("vector", lambda E, SM, RG, p8=p8,
                                         wvar=wvar: E.tensor_mul(
                            s_pb[:, 512 * p8:512 * (p8 + 1)],
                            s_pb[:, 512 * p8:512 * (p8 + 1)],
                            s_msk[:, 512 * wvar:512 * (wvar + 1)]),
                            waits=[ev_exp, ev_mload], inc=att_m)
                    return ev_exp if ev_mask is None else ev_mask

                rdy = {}
                for c in range(min(3, nch)):
                    rdy[c] = front(c)
                for c in range(nch):
                    p8 = c % 8
                    wP = [rdy.pop(c)]
                    if c == 0:
                        wP += [S.cur(att_ev), S.cur(bnd_red), S.cur(bnd_ev)]
                        if m == 0:
                            wP.append(S.cur(vtp_ev))
                            if su == 0:
                                wP.append(rope_total)
                    ev_pv = S.emit("tensor", lambda E, SM, RG, c=c, p8=p8,
                                   sp=sp, mpar=mpar, nch=nch: E.matmul(
                        P_MED[:, 512 * mpar:512 * (mpar + 1)],
                        v_chk(sp, c),
                        s_pb[:, 512 * p8:512 * (p8 + 1)],
                        start=(c == 0), stop=(c == nch - 1)),
                        waits=wP, inc=att_pv)
                    wR = []
                    if c == 0:
                        wR = [fin_prev["sc"]] if fin_prev is not None else [bcev_total]
                    ev_rs = S.emit("tensor", lambda E, SM, RG, c=c, p8=p8,
                                   nch=nch: E.matmul(
                        P_AUX[0:1, 0:512], s_oc[:],
                        s_pb[:, 512 * p8:512 * (p8 + 1)],
                        start=(c == 0), stop=(c == nch - 1)), waits=wR, inc=rs_mm)
                    gpb[p8] = ev_rs
                    if c == 0:
                        if fin_prev is not None:
                            ev_rev_prev = emit_finalize(fin_prev)
                        if su + 1 < 8 and m == 6:
                            bq = emit_bounds_side(su + 1, 0, 1, [S.cur(att_ev)])
                        if su + 1 < 8 and m == 7:
                            bk = emit_bounds_side(su + 1, 1, 0, [S.cur(att_ev)])
                            emit_bounds_tail(su + 1, bq, bk, 0)
                    if c + 3 < nch:
                        rdy[c + 3] = front(c + 3)
                ev_sc = S.emit("scalar", lambda E, SM, RG: E.activation(
                    s_rows[:, 0:512], P_AUX[0:1, 0:512], AF.Copy),
                    waits=[ev_rs], inc=rs_cp)
                S.emit("vector", lambda E, SM, RG: E.reciprocal(
                    s_rows[0:1, 512:1024], s_rows[0:1, 0:512]),
                    waits=[ev_sc], inc=rs_ve)
                ev_rb = S.emit("vector", lambda E, SM, RG: E.tensor_copy(
                    s_rbf[0:1, 0:512], s_rows[0:1, 512:1024]), inc=rs_ve)
                fin_prev = dict(rb=ev_rb, sc=ev_sc, exp3=gpx[3], pv=ev_pv,
                                mpar=mpar, aop=m % 2, su=su, m=m)
            unit_end[sp] = ev_pv
        ev_rev_prev = emit_finalize(fin_prev)

        ev_ag2 = S.emit("gpsimd", lambda E, SM, RG: E.collective_compute(
            "AllGather", ALU.bypass, replica_groups=[[0, 1, 2, 3, 4, 5, 6, 7]],
            ins=[ao_con.opt()], outs=[ao_gat.opt()]),
            waits=[S.cur(st_ao)], inc=cc_done)
        ev_wgB = S.emit("gpsimd", lambda E, SM, RG: E.collective_compute(
            "AllGather", ALU.bypass, replica_groups=[[0, 1, 2, 3, 4, 5, 6, 7]],
            ins=[wconB.opt()], outs=[wgatB.opt()]),
            waits=[S.cur(prep_st)], inc=cc_done)

        for (gi, regname) in ((0, "coffA"), (1, "coffB")):
            S.emit("sync", lambda E, SM, RG, gi=gi, regname=regname: E.dma_start(
                out=B[:].rearrange("p (h g c) -> p h g c", h=32, g=2)[:, :, gi, :],
                in_=ao_gat[ds(RG["aorow"], 4096), :].rearrange(
                    "(q p) t -> p q t", p=128)[:, :, ds(RG[regname], 512)]),
                waits=[ev_ag2, S.cur(st_x0)] if gi == 0 else [], inc=ld_attn)
        ev_att_all = S.cur(ld_attn)
        o_ev = sm("o_ev")
        oev_events = []
        for n in range(NKC):
            ev_w, sp = load_wblock("wo_t", n)
            qp = n % 2
            mmw = [ev_w, ev_att_all, S.cur(rs_cp)] + proj_psum_waits(qp)
            for k in range(NKC):
                for hf in range(2):
                    ev_mm = S.emit("tensor", lambda E, SM, RG, k=k, hf=hf, sp=sp, qp=qp: E.matmul(
                        P_BIG[:, 1024 * qp + 512 * hf:1024 * qp + 512 * (hf + 1)],
                        w_slot(sp)[:, 128 * k:128 * (k + 1)],
                        B[:, 1024 * k + 512 * hf:1024 * k + 512 * (hf + 1)],
                        start=(k == 0), stop=(k == NKC - 1)),
                        waits=mmw if (k == 0 and hf == 0) else [], inc=o_mm)
            wstate["slot_user"][sp] = ev_mm
            ev_oe = S.emit("scalar", lambda E, SM, RG, n=n, qp=qp: E.activation(
                A[:, 1024 * n:1024 * (n + 1)], P_BIG[:, 1024 * qp:1024 * (qp + 1)],
                AF.Copy), waits=[ev_mm], inc=o_ev)
            pstate["slot_user"][qp] = ev_oe
            oev_events.append(ev_oe)
        ev_bc2 = LN_emit(lambda k: A[:, 1024 * k:1024 * (k + 1)],
                         first_waits=[S.cur(o_ev)])
        for n in range(NKC):
            wx = [ev_x0sp, ev_bc2]
            if n >= 1:
                wx.append(S.cur(res1))
            ev_x0l = S.emit("sync", lambda E, SM, RG, n=n: E.dma_start(
                out=s_x0s[:, 0:1024],
                in_=x0fm_d[128 * n:128 * (n + 1), :]), waits=wx, inc=ld_x0s[0])
            S.emit("vector", lambda E, SM, RG, n=n: E.tensor_mul(
                s_sq[:, 0:1024], A[:, 1024 * n:1024 * (n + 1)], s_rb[:, 0:1024]),
                waits=[ev_bc2, ev_x0l], inc=res1)
            S.emit("vector", lambda E, SM, RG: E.tensor_sub(
                s_sq[:, 0:1024], s_sq[:, 0:1024], s_mb[:, 0:1024]), inc=res1)
            ev_r1 = S.emit("vector", lambda E, SM, RG, n=n: E.tensor_add(
                A[:, 1024 * n:1024 * (n + 1)], s_sq[:, 0:1024],
                s_x0s[:, 0:1024]), inc=res1)
            S.emit("sync", lambda E, SM, RG, n=n: E.dma_start(
                out=x1_d[128 * n:128 * (n + 1), :], in_=A[:, 1024 * n:1024 * (n + 1)]),
                waits=[ev_r1], inc=st_x1)

        ev_bc3 = LN_emit(lambda k: A[:, 1024 * k:1024 * (k + 1)],
                         first_waits=[S.cur(res1)])
        for k in range(NKC):
            ev_ln3 = LN_norm(A[:, 1024 * k:1024 * (k + 1)],
                             B[:, 1024 * k:1024 * (k + 1)], ev_bc3)

        def wgat_dmas(gat, rank_off, tstart, n_tiles, dst, waits, sem):
            done = 0
            evs = None
            first = True
            while done < n_tiles:
                t = tstart + done
                rank, pos = divmod(t, TPC)
                n_here = min(n_tiles - done, TPC - pos)
                evs = S.emit("sync", lambda E, SM, RG, rank=rank, pos=pos, n_here=n_here,
                             done=done, dst=dst, gat=gat, rank_off=rank_off: E.dma_start(
                    out=dst[:, 128 * done:128 * (done + n_here)],
                    in_=gat[128 * rank:128 * (rank + 1),
                            128 * (rank_off + pos):128 * (rank_off + pos + n_here)]),
                    waits=waits if first else [], inc=sem)
                first = False
                done += n_here
            return evs

        f1_slot_user = [None, None]
        f1_gate_user = {}
        f1_up_user = {}
        f1_hst_user = [None]

        def f1_loads(i):
            sp = i % 2
            wv_ = [ev_wgA]
            if i < 2:
                wv_ += [S.cur(norm_done), S.cur(stat_mm), S.cur(st_x1),
                        S.cur(sq_done)]
            if f1_slot_user[sp] is not None:
                wv_.append(f1_slot_user[sp])
            ev_wg = wgat_dmas(wgatA, 0, 32 * i, 32,
                              A[:, 4096 * sp:4096 * (sp + 1)], wv_, ld_fg[sp])
            ev_wu = wgat_dmas(wgatA, TPC, 32 * i, 32,
                              A[:, 8192 + 4096 * sp:8192 + 4096 * (sp + 1)],
                              [], ld_fu[sp])
            return ev_wg, ev_wu

        f1_pend = {0: f1_loads(0), 1: f1_loads(1)}
        for i in range(NIC):
            sp = i % 2
            ev_wg, ev_wu = f1_pend.pop(i)
            gbase = 1024 * (i % 2)
            mmw = [ev_wg, ev_ln3]
            for hf in range(2):
                gu = f1_gate_user.get((i % 2, hf))
                if gu is not None:
                    mmw.append(gu)
            for k in range(NKC):
                for hf in range(2):
                    ev_gmm = S.emit("tensor", lambda E, SM, RG, k=k, hf=hf, sp=sp,
                                    gbase=gbase: E.matmul(
                        P_BIG[:, gbase + 512 * hf:gbase + 512 * (hf + 1)],
                        A[:, 4096 * sp + 128 * k:4096 * sp + 128 * (k + 1)],
                        B[:, 1024 * k + 512 * hf:1024 * k + 512 * (hf + 1)],
                        start=(k == 0), stop=(k == NKC - 1)),
                        waits=mmw if (k == 0 and hf == 0) else [], inc=f1_mm)
            umw = [ev_gmm, ev_wu]
            if i < 2:
                umw.append(S.cur(att_ev))
            for hf in range(2):
                uu_ = f1_up_user.get(hf)
                if uu_ is not None:
                    umw.append(uu_)
            for k in range(NKC):
                for hf in range(2):
                    ev_umm = S.emit("tensor", lambda E, SM, RG, k=k, hf=hf, sp=sp: E.matmul(
                        P_MED[:, 512 * hf:512 * (hf + 1)],
                        A[:, 8192 + 4096 * sp + 128 * k:8192 + 4096 * sp + 128 * (k + 1)],
                        B[:, 1024 * k + 512 * hf:1024 * k + 512 * (hf + 1)],
                        start=(k == 0), stop=(k == NKC - 1)),
                        waits=umw if (k == 0 and hf == 0) else [], inc=f1_mm)
            for hf in range(2):
                ev_sil = S.emit("scalar", lambda E, SM, RG, hf=hf, gbase=gbase: E.activation(
                    s_sil[:, 512 * hf:512 * (hf + 1)],
                    P_BIG[:, gbase + 512 * hf:gbase + 512 * (hf + 1)],
                    AF.Silu), waits=[ev_umm] if hf == 0 else [], inc=f1_sil)
                f1_gate_user[(i % 2, hf)] = ev_sil
                wmul = [ev_sil]
                if f1_hst_user[0] is not None and hf == 0:
                    wmul.append(f1_hst_user[0])
                ev_h = S.emit("vector", lambda E, SM, RG, hf=hf: E.tensor_mul(
                    s_hst[:, 512 * hf:512 * (hf + 1)],
                    s_sil[:, 512 * hf:512 * (hf + 1)],
                    P_MED[:, 512 * hf:512 * (hf + 1)]),
                    waits=wmul, inc=f1_mul)
                f1_up_user[hf] = ev_h
            f1_slot_user[sp] = ev_umm
            if i + 2 < NIC:
                f1_pend[i + 2] = f1_loads(i + 2)
            ev_hdma = S.emit("gpsimd", lambda E, SM, RG, i=i: E.dma_start(
                out=h_d[i], in_=s_hst[:, 0:1024]),
                waits=[ev_h], inc=st_h)
            f1_hst_user[0] = ev_hdma

        def h_ap(i):
            if i < 64:
                return A[:, 512 * i:512 * (i + 1)]
            return B[:, 512 * (i - 64):512 * (i - 64) + 512]

        f2_slot_user = [None, None]
        f2_psum_user = [None, None]
        f2_fst_user = [None, None]
        ev_stat2 = None
        hl_evA = [None, None]
        hl_evB = [None, None]

        def f2_hloads(pas):
            wh = [S.cur(st_h)]
            if pas == 0:
                wh += [S.cur(att_pv), S.cur(f1_mm), S.cur(norm_done), S.cur(stat_mm)]
            else:
                wh += [S.cur(f2_mm)]
            hl_evA[pas] = S.emit("gpsimd", lambda E, SM, RG, pas=pas: E.dma_start(
                out=A[:].rearrange("p (n q) -> p n q", n=64),
                in_=h_d[0:64, :, 512 * pas:512 * (pas + 1)].rearrange("n p q -> p n q")),
                waits=wh, inc=ld_hA)
            hl_evB[pas] = S.emit("gpsimd", lambda E, SM, RG, pas=pas: E.dma_start(
                out=B[:, 0:11264].rearrange("p (n q) -> p n q", n=22),
                in_=h_d[64:86, :, 512 * pas:512 * (pas + 1)].rearrange("n p q -> p n q")),
                waits=[], inc=ld_hB)

        def f2_loads(b):
            pas, d = divmod(b, NKC)
            sp = d % 2
            wwd = [ev_wgB]
            if b < 2:
                wwd.append(S.cur(f1_mm))
            if f2_slot_user[sp] is not None:
                wwd.append(f2_slot_user[sp])
            wd_dst = WS[:, 0:11008] if sp == 0 else B[:, 11264:11264 + 11008]
            ev_wd = wgat_dmas(wgatB, 0, NIC * d, NIC, wd_dst, wwd, ld_wd[sp])
            return ev_wd, wd_dst

        f2_pend = {0: f2_loads(0), 1: f2_loads(1)}
        f2_hloads(0)
        for b in range(2 * NKC):
            pas, d = divmod(b, NKC)
            sp = d % 2
            if b == NKC:
                f2_hloads(1)
            ev_wd, wd_dst = f2_pend.pop(b)
            mmw = [ev_wd, hl_evA[pas]]
            if f2_psum_user[sp] is not None:
                mmw.append(f2_psum_user[sp])
            for i in range(NIC):
                wmi = mmw if i == 0 else ([hl_evB[pas]] if i == 64 else [])
                ev_dmm = S.emit("tensor", lambda E, SM, RG, i=i, sp=sp,
                                wd_dst=wd_dst: E.matmul(
                    P_MED[:, 512 * sp:512 * (sp + 1)],
                    wd_dst[:, 128 * i:128 * (i + 1)],
                    h_ap(i), start=(i == 0), stop=(i == NIC - 1)),
                    waits=wmi, inc=f2_mm)
            f2_slot_user[sp] = ev_dmm
            if b + 2 < 2 * NKC:
                f2_pend[b + 2] = f2_loads(b + 2)
            wfe = [ev_dmm, S.cur(st_fo)]
            if ev_stat2 is not None:
                wfe.append(ev_stat2)
            ev_fe = S.emit("scalar", lambda E, SM, RG, sp=sp: E.activation(
                s_fst[:, 512 * sp:512 * (sp + 1)],
                P_MED[:, 512 * sp:512 * (sp + 1)], AF.Copy),
                waits=wfe, inc=f2_ev)
            f2_psum_user[sp] = ev_fe
            ev_fq = S.emit("scalar", lambda E, SM, RG, sp=sp: E.activation(
                s_sq[:, 0:512], s_fst[:, 512 * sp:512 * (sp + 1)], AF.Square),
                inc=f2_ev)
            S.emit("tensor", lambda E, SM, RG, d=d, pas=pas, sp=sp: E.matmul(
                P_BIG[0:1, 512 * pas:512 * (pas + 1)], s_oc[:],
                s_fst[:, 512 * sp:512 * (sp + 1)],
                start=(d == 0), stop=(d == NKC - 1)),
                waits=[ev_fe], inc=stat_mm)
            ev_stat2 = S.emit("tensor", lambda E, SM, RG, d=d, pas=pas: E.matmul(
                P_BIG[0:1, 1024 + 512 * pas:1024 + 512 * (pas + 1)], s_oc[:],
                s_sq[:, 0:512], start=(d == 0), stop=(d == NKC - 1)),
                waits=[ev_fq], inc=stat_mm)
            ev_fo = S.emit("gpsimd", lambda E, SM, RG, d=d, pas=pas, sp=sp: E.dma_start(
                out=fo_d[pas, d], in_=s_fst[:, 512 * sp:512 * (sp + 1)]),
                waits=[ev_fe], inc=st_fo)
            f2_fst_user[sp] = ev_fo
        ev_bc4 = LN_finalize(ev_stat2)

        f3_user = [None, None]
        for d in range(NKC):
            sp = d % 2
            wfx = [S.cur(st_fo), S.cur(st_x1)] if d < 2 else [f3_user[sp]]
            fxb = s_fx3[sp]
            ev_lf = S.emit("sync", lambda E, SM, RG, d=d, fxb=fxb: E.dma_start(
                out=fxb[:, 0:1024].rearrange("p (a q) -> p a q", a=2),
                in_=fo_d[:, d].rearrange("a p q -> p a q")),
                waits=wfx, inc=ld_fx[sp])
            ev_lx = S.emit("sync", lambda E, SM, RG, d=d, fxb=fxb: E.dma_start(
                out=fxb[:, 1024:2048],
                in_=x1_d[128 * d:128 * (d + 1), :]), waits=[], inc=ld_fx[sp])
            wv_ = [S.cur(ld_fx[sp]), ev_bc4, S.cur(st_x2)]
            S.emit("vector", lambda E, SM, RG, fxb=fxb: E.tensor_mul(
                s_sq[:, 0:1024], fxb[:, 0:1024], s_rb[:, 0:1024]),
                waits=wv_, inc=f3_v)
            S.emit("vector", lambda E, SM, RG: E.tensor_sub(
                s_sq[:, 0:1024], s_sq[:, 0:1024], s_mb[:, 0:1024]), inc=f3_v)
            ev_x2 = S.emit("vector", lambda E, SM, RG, fxb=fxb: E.tensor_add(
                s_sq[:, 0:1024], s_sq[:, 0:1024], fxb[:, 1024:2048]), inc=f3_v)
            ev_sx2 = S.emit("sync", lambda E, SM, RG, d=d: E.dma_start(
                out=x2_d[128 * d:128 * (d + 1), :], in_=s_sq[:, 0:1024]),
                waits=[ev_x2], inc=st_x2)
            f3_user[sp] = ev_x2
        out_user = [None, None]
        for tc in range(NT):
            sp = tc % 2
            w = [S.cur(st_x2)]
            if out_user[sp] is not None:
                w.append(S.cur(st_out))
            ev_tl = S.emit("sync", lambda E, SM, RG, tc=tc, sp=sp: E.dma_start_transpose(
                WS[:, 4096 * sp:4096 * (sp + 1)], x2_d[:, 128 * tc:128 * (tc + 1)]),
                waits=w, inc=ld_out[sp])
            ev_so = S.emit("gpsimd", lambda E, SM, RG, tc=tc, sp=sp: E.dma_start(
                out=out_d[128 * tc:128 * (tc + 1), :],
                in_=WS[:, 4096 * sp:4096 * (sp + 1)]),
                waits=[ev_tl], inc=st_out)
            out_user[sp] = ev_so
        S.emit("gpsimd", lambda E, SM, RG: E.nop(), waits=[S.cur(st_out)])

        st2 = ExitStack()
        with st2:
            sems = {name: st2.enter_context(nc.semaphore(name)) for name in S.plan}
            with nc.Block() as block:

                @block.sync
                def _(E):
                    pid = E.partition_id()
                    rho = E.compute_val(pid % 4)
                    bat = E.compute_val(pid // 4)
                    RG = {}
                    RG["coffA"] = E.compute_val(rho * 512)
                    RG["coffB"] = E.compute_val(rho * 512 + 2048)
                    RG["aorow"] = E.compute_val(bat * 4096)
                    S.run("sync", E, sems, RG)

                @block.gpsimd
                def _(E):
                    S.run("gpsimd", E, sems, {})

                @block.tensor
                def _(E):
                    S.run("tensor", E, sems, {})

                @block.vector
                def _(E):
                    S.run("vector", E, sems, {})

                @block.scalar
                def _(E):
                    S.run("scalar", E, sems, {})

    return nc


_NC_CACHE = None


def kernel(**inputs):
    global _NC_CACHE
    in_maps = _host_prep(inputs)
    if _NC_CACHE is None:
        _NC_CACHE = _build()
    res = run_bass_kernel_spmd(_NC_CACHE, in_maps, core_ids=list(range(8)))
    out = np.zeros((BATCH, T, C), dtype=np.float32)
    for c in range(8):
        b, j = c // 4, c % 4
        o = np.asarray(res.results[c]["out"], dtype=np.float32)
        out[b, 512 * j:512 * (j + 1)] = o[0:512]
        out[b, 512 * (j + 4):512 * (j + 5)] = o[512:1024]
    return out
